# version 04 (final)
# baseline (speedup 1.0000x reference)
import math
import numpy as np
import ml_dtypes
from contextlib import ExitStack

import concourse.bass as bass
import concourse.mybir as mybir
from concourse.bass_utils import run_bass_kernel_spmd

F32 = mybir.dt.float32
BF16 = mybir.dt.bfloat16
I32 = mybir.dt.int32
AF = mybir.ActivationFunctionType
ALU = mybir.AluOpType
AX = mybir.AxisListType

D = 1024
T = 2048
NCH = 8
NTC = 16
NT512 = 4


def _rope_tables():
    HD = 64
    inv_freq = 1.0 / (10000.0 ** (np.arange(0, HD, 2, dtype=np.float64) / HD))
    t = np.arange(T, dtype=np.float64)
    dgrid = np.arange(HD)
    freq_of_d = inv_freq[dgrid % 32]
    perm = np.concatenate([np.arange(0, HD, 2), np.arange(1, HD, 2)])
    fr = freq_of_d[perm]
    ang = np.outer(fr, t)
    cos = np.cos(ang).astype(np.float32)
    sin = np.sin(ang).astype(np.float32)
    return np.vstack([cos, cos]).copy(), np.vstack([sin, sin]).copy()


def host_consts():
    c = {}
    i = np.arange(128)
    c["c_ident"] = np.eye(128, dtype=np.float32)
    c["c_upper01"] = (i[:, None] <= i[None, :]).astype(np.float32)
    c["c_supper01"] = (i[:, None] > i[None, :]).astype(np.float32)
    c["c_maskneg"] = np.where(i[:, None] <= i[None, :], 0.0, -1e30).astype(np.float32)
    c["c_ones_col"] = np.ones((1, 128), dtype=np.float32)
    h0 = np.zeros((1, 128), dtype=np.float32); h0[0, 0:64] = 1.0
    h1 = np.zeros((1, 128), dtype=np.float32); h1[0, 64:128] = 1.0
    c["c_half0"] = h0
    c["c_half1"] = h1
    mc = np.zeros((128, 1), dtype=np.float32); mc[0:64] = 1.0
    c["c_maskeven"] = mc
    c["c_negones"] = -np.ones((1, 128), dtype=np.float32)
    dm = np.zeros((1, 32), dtype=np.float32); dm[0, 0:28] = -1.0
    am = np.zeros((1, 32), dtype=np.float32); am[0, 28:32] = -1.0
    c["c_dmask"] = dm
    c["c_amask"] = am
    pos = np.arange(2048, dtype=np.float32)
    dp = np.zeros((2, 2048), dtype=np.float32)
    dp[0] = np.maximum(2047.0 - pos, 0.0)
    dp[1] = pos
    c["c_distpos"] = dp
    c["c_maskodd"] = np.ascontiguousarray(1.0 - mc)
    c["c_ones_128"] = np.ones((128, 1), dtype=np.float32)
    cos, sin = _rope_tables()
    c["c_rope_cos"] = cos
    c["c_rope_sin"] = sin
    Pm = np.zeros((64, 64), dtype=np.float32)
    for r in range(32):
        Pm[r, 32 + r] = -1.0
        Pm[32 + r, r] = 1.0
    blk = np.zeros((128, 128), dtype=np.float32)
    blk[0:64, 0:64] = Pm
    blk[64:128, 64:128] = Pm
    c["c_ropeP"] = np.ascontiguousarray(blk.T)
    pos = np.arange(T, dtype=np.float32)
    c["c_pos"] = pos.reshape(1, T).copy()
    c["c_dist"] = np.maximum(float(T - 1) - pos, 0.0).reshape(1, T).copy()
    return c

CONSTS = host_consts()
DEBUG = False
PARANOID = (0, 10**9)
L1B_TRUNC = {"pw": True, "ntc": 16, "tail": True}


class Prog:
    def __init__(self, nc):
        self.nc = nc
        self.stack = ExitStack()
        self.counts = {}
        self.marks = {}
        self._sems = {}
        self.blk = None
        self.names = set()

    def sem(self, name):
        if name not in self._sems:
            self._sems[name] = self.stack.enter_context(self.nc.semaphore(name))
            self.counts[name] = 0
        return self._sems[name]

    def _uniq(self, name):
        base, k = name, 0
        while name in self.names:
            k += 1
            name = f"{base}_{k}"
        self.names.add(name)
        return name

    def sb(self, name, shape, dt=F32):
        return self.stack.enter_context(self.nc.sbuf_tensor(self._uniq(name), shape, dt))

    def ps(self, name, shape, dt=F32):
        return self.stack.enter_context(self.nc.psum_tensor(self._uniq(name), shape, dt))

    def begin(self):
        for e in ("sync", "scalar", "vector", "gpsimd", "tensor", "cc"):
            self.sem("s_" + e)
        self.blk = self.stack.enter_context(self.nc.Block())

    def tick(self, instr, engname, inc=1):
        name = "s_" + engname
        instr.then_inc(self._sems[name], inc)
        self.counts[name] += inc
        return (name, self.counts[name])

    def tok(self, engname):
        name = "s_" + engname
        return (name, self.counts[name])

    def wait(self, eng, engname, *tokens):
        best = {}
        if globals().get("DEBUG"):
            tokens = tokens + tuple(getattr(self, "dbg_toks", []))

        def add(t):
            if t is None:
                return
            if isinstance(t, (list, tuple)):
                if not t or isinstance(t[0], (list, tuple)) or t[0] is None:
                    for u in t:
                        add(u)
                    return
            best[t[0]] = max(best.get(t[0], 0), t[1])
        for t in tokens:
            add(t)
        for s, v in sorted(best.items()):
            if s == "s_" + engname and engname not in ("sync",):
                if s not in ("s_sync", "s_gpsimd", "s_scalar"):
                    continue
            key = (engname, s)
            if self.marks.get(key, -1) >= v:
                continue
            eng.wait_ge(self._sems[s], v)
            self.marks[key] = v

    def chain(self, eng, engname, instr):
        tok = self.tick(instr, engname)
        eng.wait_ge(self._sems[tok[0]], tok[1])
        return tok

    SEC_N = [0]

    def sec(self, engname, fn):
        lo, hi = globals().get("PARANOID", (None, None)) or (None, None)
        idx = Prog.SEC_N[0]
        Prog.SEC_N[0] += 1

        if lo is not None and lo <= idx < hi and engname != "cc":
            def wrapped(eng):
                for s in ("s_sync", "s_scalar", "s_vector", "s_gpsimd", "s_tensor"):
                    v = self.counts[s]
                    if v > 0:
                        key = (engname, s)
                        if self.marks.get(key, -1) < v:
                            eng.wait_ge(self._sems[s], v)
                            self.marks[key] = v
                fn(eng)
            getattr(self.blk, engname)(wrapped)
        else:
            getattr(self.blk, engname)(fn)

    def dbg_dump(self, name, ap, waits):
        if not globals().get("DEBUG"):
            return
        shp = list(ap.shape)
        out = self.nc.dram_tensor("dbg_" + name, shp, ap.dtype, kind="ExternalOutput").ap()
        toks = []

        def _(g):
            toks.append(self.dma(g, "gpsimd", out, ap, waits=waits))
        self.sec("gpsimd", _)
        self.dbg_toks = getattr(self, "dbg_toks", [])
        self.dbg_toks.append(toks[-1])

    def dma(self, q, qname, out, in_, waits=(), **kw):
        self.wait(q, qname, *waits)
        ins = q.dma_start(out=out, in_=in_, **kw)
        return self.tick(ins, qname, 16)


class Banks:
    def __init__(self, P, ap, nbanks, width=512):
        self.P = P
        self.ap = ap
        self.n = nbanks
        self.width = width
        self.guard = [None] * nbanks
        self.idx = 0

    def slot(self, j):
        return self.ap[:, j * self.width:(j + 1) * self.width]

    def next(self):
        j = self.idx % self.n
        self.idx += 1
        return j



CONST_SPEC = {
    "ident": ("c_ident", F32), "ident_bf": ("c_ident", BF16),
    "upper01": ("c_upper01", BF16), "supper01": ("c_supper01", BF16),
    "maskneg": ("c_maskneg", F32),
    "triu_bf": ("c_upper01", BF16), "triu_f": ("c_upper01", F32),
    "ones_col": ("c_ones_col", F32), "ones_col_bf": ("c_ones_col", BF16),
    "ones_128": ("c_ones_128", BF16), "ones_128f": ("c_ones_128", F32),
    "ropeP": ("c_ropeP", BF16),
    "half0": ("c_half0", BF16), "half1": ("c_half1", BF16),
    "maskeven": ("c_maskeven", F32), "maskodd": ("c_maskodd", F32),
    "negones": ("c_negones", F32), "triu_f32": ("c_upper01", F32),
    "ident_bf2": ("c_ident", BF16),
    "dmask": ("c_dmask", F32), "amask": ("c_amask", F32),
    "distpos": ("c_distpos", F32),
    "rope_cos": ("c_rope_cos", BF16), "rope_sin": ("c_rope_sin", BF16),
    "pos": ("c_pos", F32), "dist": ("c_dist", F32),
}

LN_CONSTS = ["ones_128", "ones_128f", "ones_col_bf"]


def emit_consts(P, cin, names):
    cs = {}
    spec = [(nm,) + CONST_SPEC[nm] for nm in names]
    for nm, src, dt in spec:
        cs[nm] = P.sb(nm, list(cin[src].shape), dt)
    toks = []

    def _(g):
        for nm, src, dt in spec:
            toks.append(P.dma(g, "gpsimd", cs[nm][:], cin[src]))
    P.sec("gpsimd", _)
    cs["tok"] = toks[-1]
    return cs


def emit_ln(P, cs, sc, tag):
    xsrc = sc["xsrc"]
    xn = sc["xn"]
    PA = sc["PA"]
    PB = sc["PB"]
    stage = sc["stage"]
    xsq = sc["t_bf"]
    R0, R1 = sc["R0"], sc["R1"]
    Ab, Bb = sc["Rb0"], sc["Rb1"]
    t_mm = []
    ld_guard = [None]

    xg = [None]
    for c in range(NCH):
        ld = []

        def _(sy, c=c):
            ld.append(P.dma(sy, "sync", stage[:], xsrc[:, c * T:(c + 1) * T],
                            waits=(ld_guard[0],)))
        P.sec("sync", _)

        def _(v, c=c):
            P.wait(v, "vector", ld[-1], xg[0])
            ins = v.tensor_scalar(out=xsq[:, 0:T], in0=stage[:], scalar1=0.0,
                                  scalar2=None, op0=ALU.add, op1=ALU.bypass)
            xg[0] = P.tick(ins, "vector")
        P.sec("vector", _)

        def _(t, c=c):
            P.wait(t, "tensor", ld[-1], xg[0], sc.get("psA_free"))
            for j in range(4):
                ins = t.matmul(PA[0:1, j * 512:(j + 1) * 512], lhsT=cs["ones_128"][:],
                               rhs=xsq[:, j * 512:(j + 1) * 512],
                               start=(c == 0), stop=(c == NCH - 1))
            tk = P.tick(ins, "tensor")
            if c == NCH - 1:
                t_mm.append(tk)
            ld_guard[0] = tk
            xg[0] = tk
        P.sec("tensor", _)

    def _(v):
        P.wait(v, "vector", t_mm)
        ins = v.tensor_scalar(out=R0[:], in0=PA[0:1, 0:T], scalar1=1.0 / D,
                              scalar2=None, op0=ALU.mult, op1=ALU.bypass)
        t_mm.append(P.tick(ins, "vector"))
    P.sec("vector", _)

    t_mm2p = []
    for c in range(NCH):
        ld = []

        def _(sy, c=c):
            ld.append(P.dma(sy, "sync", stage[:], xsrc[:, c * T:(c + 1) * T],
                            waits=(ld_guard[0],)))
        P.sec("sync", _)
        t_sq = []

        def _(s, c=c):
            P.wait(s, "scalar", ld[-1], xg[0])
            ins = s.activation(out=xsq[:, 0:T], in_=stage[:], func=AF.Square)
            t_sq.append(P.tick(ins, "scalar"))
        P.sec("scalar", _)

        def _(t, c=c):
            P.wait(t, "tensor", ld[-1], t_sq[-1], t_mm[-1], xg[0])
            for j in range(4):
                ins = t.matmul(PA[0:1, j * 512:(j + 1) * 512], lhsT=cs["ones_128"][:],
                               rhs=xsq[:, j * 512:(j + 1) * 512],
                               start=(c == 0), stop=(c == NCH - 1))
            tk = P.tick(ins, "tensor")
            if c == NCH - 1:
                t_mm2p.append(tk)
            ld_guard[0] = tk
            xg[0] = tk
        P.sec("tensor", _)

    t_rows = []

    def _(v):
        P.wait(v, "vector", t_mm2p)
        ins = v.tensor_scalar(out=R1[:], in0=PA[0:1, 0:T], scalar1=1.0 / D,
                              scalar2=None, op0=ALU.mult, op1=ALU.bypass)
        P.chain(v, "vector", ins)
        ins = v.tensor_tensor(out=stage[0:1, 0:T], in0=R0[:], in1=R0[:], op=ALU.mult)
        P.chain(v, "vector", ins)
        ins = v.tensor_tensor(out=R1[:], in0=R1[:], in1=stage[0:1, 0:T], op=ALU.subtract)
        P.chain(v, "vector", ins)
        ins = v.tensor_scalar(out=R1[:], in0=R1[:], scalar1=1e-5, scalar2=None,
                              op0=ALU.add, op1=ALU.bypass)
        P.chain(v, "vector", ins)
        ins = v.reciprocal(out=R1[:], in_=R1[:])
        t_rows.append(P.tick(ins, "vector"))
    P.sec("vector", _)

    def _(s):
        P.wait(s, "scalar", t_rows)
        ins = s.activation(out=R1[:], in_=R1[:], func=AF.Sqrt)
        t_rows.append(P.tick(ins, "scalar"))
    P.sec("scalar", _)

    t_ab = []

    def _(v):
        P.wait(v, "vector", t_rows)
        ins = v.scalar_tensor_tensor(out=R0[:], in0=R0[:], scalar=-1.0, in1=R1[:],
                                     op0=ALU.mult, op1=ALU.mult)
        P.chain(v, "vector", ins)
        ins = v.tensor_scalar(out=Ab[:], in0=R1[:], scalar1=0.0, scalar2=None,
                              op0=ALU.add, op1=ALU.bypass)
        ins = v.tensor_scalar(out=Bb[:], in0=R0[:], scalar1=0.0, scalar2=None,
                              op0=ALU.add, op1=ALU.bypass)
        t_ab.append(P.tick(ins, "vector"))
    P.sec("vector", _)
    sc["psA_free"] = t_ab[-1]

    P.dbg_dump("mean", R0[:], t_mm)
    P.dbg_dump("Asmall", R1[:], t_rows)
    abc_s = sc["abc_s"]
    bbc_s = sc["bbc_s"]
    bc_toks = []
    for j in range(NT512):
        t_mm2 = []

        def _(t, j=j):
            P.wait(t, "tensor", t_ab, sc.get("psB_guard"))
            t.matmul(PB[:, 0:512], lhsT=cs["ones_col_bf"][:],
                     rhs=Ab[0:1, j * 512:(j + 1) * 512], start=True, stop=True)
            ins = t.matmul(PB[:, 512:1024], lhsT=cs["ones_col_bf"][:],
                           rhs=Bb[0:1, j * 512:(j + 1) * 512], start=True, stop=True)
            t_mm2.append(P.tick(ins, "tensor"))
        P.sec("tensor", _)

        def _(s, j=j, t_mm2=t_mm2):
            P.wait(s, "scalar", t_mm2[-1])
            s.activation(out=abc_s[:, j * 512:(j + 1) * 512], in_=PB[:, 0:512], func=AF.Copy)
            ins = s.activation(out=bbc_s[:, j * 512:(j + 1) * 512], in_=PB[:, 512:1024],
                               func=AF.Copy)
            tk = P.tick(ins, "scalar")
            bc_toks.append(tk)
            sc["psB_guard"] = tk
        P.sec("scalar", _)

    P.dbg_dump("abc", abc_s[:], bc_toks)
    P.dbg_dump("bbc", bbc_s[:], bc_toks)
    ld_guard[0] = [ld_guard[0], t_rows[0]] if ld_guard[0] else t_rows[0]
    xn_toks = []
    for c in range(NCH):
        ld = []

        def _(sy, c=c):
            ld.append(P.dma(sy, "sync", stage[:], xsrc[:, c * T:(c + 1) * T],
                            waits=(ld_guard[0],)))
        P.sec("sync", _)

        def _(v, c=c):
            P.wait(v, "vector", ld[-1], bc_toks)
            tmp = sc["t_f32"]
            last = None
            for j in range(NT512):
                jsl = slice(j * 512, (j + 1) * 512)
                ins = v.tensor_tensor(out=tmp[:, 0:512], in0=stage[:, jsl],
                                      in1=abc_s[:, jsl], op=ALU.mult)
                P.chain(v, "vector", ins)
                ins = v.tensor_tensor(out=xn[:, c * T + j * 512: c * T + (j + 1) * 512],
                                      in0=tmp[:, 0:512], in1=bbc_s[:, jsl], op=ALU.add)
                last = P.tick(ins, "vector")
            xn_toks.append(last)
            ld_guard[0] = last
        P.sec("vector", _)
    sc["g_stage"] = xn_toks[-1]
    sc["g_tbf"] = t_mm2p[-1]
    return xn_toks



def emit_attn_b(P, cs, sc, W):
    xn = sc["xn"]
    PA = sc["PA"]
    PB = sc["PB"]

    xn_toks = emit_ln(P, cs, sc, "lnb")
    P.dbg_dump("xn", sc["xn"][:], xn_toks)

    w_s = P.sb("w_s", [128, NCH * D], BF16)
    q_s = P.sb("q_s", [128, NCH * T], BF16)
    k_s = P.sb("k_s", [128, NCH * T], BF16)
    vt_s = P.sb("vt_s", [128, NTC * D], BF16)

    evac_toks = {}
    w_guard = [None]

    def load_w(src):
        wt = []

        def _(g):
            wt.append(P.dma(g, "gpsimd", w_s[:].rearrange("p (c n) -> p c n", c=NCH),
                            src.rearrange("(c p) n -> p c n", c=NCH),
                            waits=(w_guard[0],)))
        P.sec("gpsimd", _)
        return wt[-1]

    for wi, (src, os_, nm) in enumerate(((W["wq"], q_s, "q"), (W["wk"], k_s, "k"))):
        wtok = load_w(src)
        for m in range(NCH):
            for j in range(NT512):
                bidx = (m * NT512 + j) % 4
                bank = PA[:, bidx * 512:(bidx + 1) * 512]
                mm_tok = []

                def _(t, m=m, j=j, bank=bank, wtok=wtok, bidx=bidx, nm=nm):
                    P.wait(t, "tensor", wtok, xn_toks, sc.get("psA_free"),
                           evac_toks.get((nm, bidx)))
                    for c in range(NCH):
                        ins = t.matmul(bank,
                                       lhsT=w_s[:, c * D + m * 128: c * D + (m + 1) * 128],
                                       rhs=xn[:, c * T + j * 512: c * T + (j + 1) * 512],
                                       start=(c == 0), stop=(c == NCH - 1))
                    mm_tok.append(P.tick(ins, "tensor"))
                P.sec("tensor", _)

                def _(s, m=m, j=j, bank=bank, os_=os_, mm_tok=mm_tok, bidx=bidx, nm=nm):
                    P.wait(s, "scalar", mm_tok[-1])
                    ins = s.activation(out=os_[:, m * T + j * 512: m * T + (j + 1) * 512],
                                       in_=bank, func=AF.Copy)
                    evac_toks[(nm, bidx)] = P.tick(ins, "scalar")
                P.sec("scalar", _)
        w_guard[0] = P.tok("tensor")
    sc["psA_free"] = [evac_toks[("q", i)] for i in range(4)] + \
                     [evac_toks[("k", i)] for i in range(4)]

    wtok = load_w(W["wv"])
    for tc in range(NTC):
        for half in range(2):
            bidx = (tc * 2 + half) % 4
            bank = PA[:, bidx * 512:(bidx + 1) * 512]
            mm_tok = []

            def _(t, tc=tc, half=half, bank=bank, bidx=bidx, wtok=wtok):
                P.wait(t, "tensor", wtok, xn_toks, evac_toks.get(("v", bidx)),
                       sc.get("psA_free"))
                for c in range(NCH):
                    ins = t.matmul(bank,
                                   lhsT=xn[:, c * T + tc * 128: c * T + (tc + 1) * 128],
                                   rhs=w_s[:, c * D + half * 512: c * D + (half + 1) * 512],
                                   start=(c == 0), stop=(c == NCH - 1))
                mm_tok.append(P.tick(ins, "tensor"))
            P.sec("tensor", _)

            def _(s, tc=tc, half=half, bank=bank, mm_tok=mm_tok, bidx=bidx):
                P.wait(s, "scalar", mm_tok[-1])
                ins = s.activation(out=vt_s[:, tc * D + half * 512: tc * D + (half + 1) * 512],
                                   in_=bank, func=AF.Copy)
                evac_toks[("v", bidx)] = P.tick(ins, "scalar")
            P.sec("scalar", _)
    sc["psA_free"] = [evac_toks[("v", i)] for i in range(4)]
    w_guard[0] = P.tok("tensor")
    P.dbg_dump("q_s", q_s[:], [P.tok("scalar")])
    P.dbg_dump("k_s", k_s[:], [P.tok("scalar")])
    P.dbg_dump("R1", sc["R1"][:], [P.tok("scalar")])

    nsq = sc["t_bf"]
    negc0col = P.sb("negc0col", [128, 1], F32)
    r1 = P.sb("nr1", [1, 2], F32)
    negc0b = P.sb("negc0b", [1, 1], BF16)
    t_c0 = []
    nsq_g = [None]
    nsq_final = [None]
    for wi, srcb in enumerate((q_s, k_s)):
        t_nmm = []
        for c in range(NCH):
            t_sq2 = []

            def _(s, srcb=srcb, c=c):
                P.wait(s, "scalar", t_c0, nsq_g[0], sc.get("g_tbf"))
                ins = s.activation(out=nsq[:, 0:T], in_=srcb[:, c * T:(c + 1) * T],
                                   func=AF.Square)
                t_sq2.append(P.tick(ins, "scalar"))
            P.sec("scalar", _)

            def _(t, c=c, t_sq2=t_sq2):
                P.wait(t, "tensor", t_sq2[-1], sc.get("psA_free"), t_c0)
                for j in range(NT512):
                    ins = t.matmul(PA[0:1, j * 512:(j + 1) * 512],
                                   lhsT=cs["ones_128"][:],
                                   rhs=nsq[:, j * 512:(j + 1) * 512],
                                   start=(c == 0), stop=(c == NCH - 1))
                tk = P.tick(ins, "tensor")
                nsq_g[0] = tk
                nsq_final[0] = tk
                if c == NCH - 1:
                    t_nmm.append(tk)
            P.sec("tensor", _)

        def _(v, wi=wi, t_nmm=t_nmm):
            P.wait(v, "vector", t_nmm)
            ins = v.tensor_reduce(out=r1[0:1, wi:wi + 1], in_=PA[0:1, 0:T],
                                  axis=AX.X, op=ALU.max)
            t_c0.append(P.tick(ins, "vector"))
            ins = v.tensor_scalar(out=sc["R1"][:], in0=PA[0:1, 0:T], scalar1=1.0,
                                  scalar2=None, op0=ALU.mult, op1=ALU.bypass)
            t_c0.append(P.tick(ins, "vector"))
        P.sec("vector", _)
        P.dbg_dump("normrow%d" % wi, sc["R1"][:], t_c0)

    def _(s):
        P.wait(s, "scalar", t_c0)
        ins = s.activation(out=r1[:], in_=r1[:], func=AF.Sqrt)
        t_c0.append(P.tick(ins, "scalar"))
    P.sec("scalar", _)

    def _(v):
        P.wait(v, "vector", t_c0)
        ins = v.tensor_tensor(out=r1[0:1, 0:1], in0=r1[0:1, 0:1], in1=r1[0:1, 1:2],
                              op=ALU.mult)
        P.chain(v, "vector", ins)
        ins = v.tensor_scalar(out=negc0b[:], in0=r1[0:1, 0:1], scalar1=-(D ** -0.5),
                              scalar2=None, op0=ALU.mult, op1=ALU.bypass)
        t_c0.append(P.tick(ins, "vector"))
    P.sec("vector", _)
    sc["psA_free"] = t_c0[-1]

    def _(t):
        P.wait(t, "tensor", t_c0, sc.get("psB_guard"))
        ins = t.matmul(PB[:, 0:1], lhsT=cs["ones_col_bf"][:], rhs=negc0b[:],
                       start=True, stop=True)
        t_c0.append(P.tick(ins, "tensor"))
    P.sec("tensor", _)

    def _(s):
        P.wait(s, "scalar", t_c0[-1])
        ins = s.activation(out=negc0col[:], in_=PB[:, 0:1], func=AF.Copy)
        t_c0.append(P.tick(ins, "scalar"))
    P.sec("scalar", _)
    c0_tok = t_c0[-1]
    P.dbg_dump("r1", r1[:], t_c0)
    sc["psB_guard"] = c0_tok
    sc["g_tbf"] = nsq_final[0]

    ybuf = xn
    den_s = sc["R0"]
    pts = P.sb("pts", [128, NTC * 128], BF16)
    y_ev = {"all": None}
    for qt in range(NTC):
        exp_toks = {}
        mask_toks = {}
        av_tok = []
        den_tok = []
        for kt in range(qt + 1):
            sidx = 2 + (kt % 2)
            pS = PA[:, sidx * 512: sidx * 512 + 128]
            st_tok = []

            def _(t, qt=qt, kt=kt, pS=pS, sidx=sidx):
                P.wait(t, "tensor", c0_tok, exp_toks.get(sidx), y_ev.get("all"))
                for c in range(NCH):
                    ins = t.matmul(pS, lhsT=k_s[:, c * T + kt * 128: c * T + (kt + 1) * 128],
                                   rhs=q_s[:, c * T + qt * 128: c * T + (qt + 1) * 128],
                                   start=(c == 0), stop=(c == NCH - 1))
                st_tok.append(P.tick(ins, "tensor"))
            P.sec("tensor", _)

            def _(s, qt=qt, kt=kt, pS=pS, sidx=sidx, st_tok=st_tok):
                P.wait(s, "scalar", st_tok[-1])
                ins = s.activation(out=pts[:, kt * 128:(kt + 1) * 128], in_=pS,
                                   func=AF.Exp, scale=D ** -0.5, bias=negc0col[:, 0:1])
                exp_toks[sidx] = P.tick(ins, "scalar")
            P.sec("scalar", _)

            if kt == qt:
                def _(v, qt=qt, kt=kt, sidx=sidx):
                    P.wait(v, "vector", exp_toks[sidx])
                    ins = v.tensor_tensor(out=pts[:, kt * 128:(kt + 1) * 128],
                                          in0=pts[:, kt * 128:(kt + 1) * 128],
                                          in1=cs["upper01"][:], op=ALU.mult)
                    mask_toks[kt] = P.tick(ins, "vector")
                P.sec("vector", _)

            def _(t, qt=qt, kt=kt, sidx=sidx):
                P.wait(t, "tensor", exp_toks[sidx], mask_toks.get(kt))
                pt = pts[:, kt * 128:(kt + 1) * 128]
                ins = t.matmul(PB[0:1, 0:128], lhsT=cs["ones_128"][:], rhs=pt,
                               start=(kt == 0), stop=(kt == qt))
                if kt == qt:
                    den_tok.append(P.tick(ins, "tensor"))
                for c in range(NCH):
                    ins = t.matmul(PA[:, c * 128:(c + 1) * 128],
                                   lhsT=vt_s[:, kt * D + c * 128: kt * D + (c + 1) * 128],
                                   rhs=pt, start=(kt == 0 and c % 4 == 0), stop=False)
                if kt == qt:
                    av_tok.append(P.tick(ins, "tensor"))
            P.sec("tensor", _)

        def _(s, qt=qt, av_tok=av_tok, den_tok=den_tok):
            P.wait(s, "scalar", av_tok, den_tok)
            toks = [P.tick(s.activation(out=den_s[0:1, qt * 128:(qt + 1) * 128],
                                        in_=PB[0:1, 0:128], func=AF.Copy), "scalar")]
            for c in range(NCH):
                ins = s.activation(out=ybuf[:, c * T + qt * 128: c * T + (qt + 1) * 128],
                                   in_=PA[:, c * 128:(c + 1) * 128], func=AF.Copy)
                toks.append(P.tick(ins, "scalar"))
            y_ev["all"] = toks
        P.sec("scalar", _)

    rdenb = sc["Rb0"]
    rb_s = sc["abc_s"]
    t_r = []

    def _(v):
        P.wait(v, "vector", y_ev["all"])
        ins = v.reciprocal(out=den_s[:], in_=den_s[:])
        P.chain(v, "vector", ins)
        ins = v.tensor_scalar(out=rdenb[:], in0=den_s[:], scalar1=0.0, scalar2=None,
                              op0=ALU.add, op1=ALU.bypass)
        t_r.append(P.tick(ins, "vector"))
    P.sec("vector", _)

    rb_toks = []
    for j in range(NT512):
        t_mm2 = []

        def _(t, j=j):
            P.wait(t, "tensor", t_r, sc.get("psB_guard"))
            ins = t.matmul(PB[:, 512:1024], lhsT=cs["ones_col_bf"][:],
                           rhs=rdenb[0:1, j * 512:(j + 1) * 512], start=True, stop=True)
            t_mm2.append(P.tick(ins, "tensor"))
        P.sec("tensor", _)

        def _(s, j=j, t_mm2=t_mm2):
            P.wait(s, "scalar", t_mm2[-1])
            ins = s.activation(out=rb_s[:, j * 512:(j + 1) * 512], in_=PB[:, 512:1024],
                               func=AF.Copy)
            tk = P.tick(ins, "scalar")
            rb_toks.append(tk)
            sc["psB_guard"] = tk
        P.sec("scalar", _)

    sc["dbg"] = {"den": den_s[:], "negc0": negc0col[:], "ybuf": ybuf[:]}
    wtok = load_w(W["wo"])
    return emit_proj_residual(P, cs, sc, w_s, ybuf, wtok, rb_s, rb_toks)


def emit_proj_residual(P, cs, sc, w_s, ybuf, wtok, scale_s=None, scale_toks=None,
                       nk=NCH):
    PA = sc["PA"]
    xsrc = sc["xsrc"]
    xdst = sc["xdst"]
    stage = sc["stage"]
    res_toks = []
    ev_guard = {}
    st_guard = [None]
    for m in range(NCH):
        sl = slice(0, T)
        ld = []

        def _(sy, m=m, sl=sl):
            ld.append(P.dma(sy, "sync", stage[:, sl], xsrc[:, m * T:(m + 1) * T],
                            waits=(st_guard[0], sc.get("g_stage"))))
        P.sec("sync", _)
        add_toks = []
        for j in range(NT512):
            bidx = (m * NT512 + j) % 4
            bank = PA[:, bidx * 512:(bidx + 1) * 512]
            mm_tok = []

            def _(t, m=m, j=j, bank=bank, bidx=bidx):
                P.wait(t, "tensor", wtok, scale_toks, ev_guard.get(bidx),
                       sc.get("psA_free"))
                for c in range(nk):
                    ins = t.matmul(bank, lhsT=w_s[:, c * D + m * 128: c * D + (m + 1) * 128],
                                   rhs=ybuf[:, c * T + j * 512: c * T + (j + 1) * 512],
                                   start=(c == 0), stop=(c == nk - 1))
                mm_tok.append(P.tick(ins, "tensor"))
            P.sec("tensor", _)

            def _(v, m=m, j=j, bank=bank, mm_tok=mm_tok, bidx=bidx, sl=sl, ld=ld):
                P.wait(v, "vector", mm_tok[-1], ld[-1])
                tmp = sc["t_f32"]
                ssl = slice(j * 512, (j + 1) * 512)
                if scale_s is not None:
                    ins = v.tensor_tensor(out=tmp[:, 0:512], in0=bank, in1=scale_s[:, ssl],
                                          op=ALU.mult)
                    P.chain(v, "vector", ins)
                    ins = v.tensor_tensor(out=stage[:, sl][:, ssl], in0=stage[:, sl][:, ssl],
                                          in1=tmp[:, 0:512], op=ALU.add)
                else:
                    ins = v.tensor_tensor(out=stage[:, sl][:, ssl], in0=stage[:, sl][:, ssl],
                                          in1=bank, op=ALU.add)
                tk = P.tick(ins, "vector")
                ev_guard[bidx] = tk
                add_toks.append(tk)
            P.sec("vector", _)

        def _(g, m=m, sl=sl, add_toks=add_toks):
            tk = P.dma(g, "gpsimd", xdst[:, m * T:(m + 1) * T], stage[:, sl],
                       waits=add_toks)
            st_guard[0] = tk
            res_toks.append(tk)
        P.sec("gpsimd", _)
    return res_toks



def emit_negc0_perchunk(P, cs, sc, q_s, k_s, scale, tag):
    PA = sc["PA"]
    PB = sc["PB"]
    nsq = sc["t_bf"]
    r8 = P.sb(tag + "_r8", [1, 16], F32)
    negb8 = P.sb(tag + "_nb8", [1, 8], BF16)
    negc08 = P.sb(tag + "_nc08", [128, 8], F32)
    t_c0 = []
    nsq_g = [sc.get("g_tbf")]
    for wi, srcb in enumerate((q_s, k_s)):
        for c in range(NCH):
            t_sq2 = []

            def _(s, srcb=srcb, c=c):
                P.wait(s, "scalar", t_c0, nsq_g[0])
                ins = s.activation(out=nsq[:, 0:T], in_=srcb[:, c * T:(c + 1) * T],
                                   func=AF.Square)
                t_sq2.append(P.tick(ins, "scalar"))
            P.sec("scalar", _)
            t_nm = []

            def _(t, c=c, t_sq2=t_sq2):
                P.wait(t, "tensor", t_sq2[-1], sc.get("psA_free"), t_c0)
                for j in range(NT512):
                    ins = t.matmul(PA[0:1, j * 512:(j + 1) * 512],
                                   lhsT=cs["ones_128"][:],
                                   rhs=nsq[:, j * 512:(j + 1) * 512],
                                   start=True, stop=True)
                tk = P.tick(ins, "tensor")
                nsq_g[0] = tk
                t_nm.append(tk)
            P.sec("tensor", _)

            def _(v, wi=wi, c=c, t_nm=t_nm):
                P.wait(v, "vector", t_nm[-1])
                ins = v.tensor_reduce(out=r8[0:1, wi * 8 + c: wi * 8 + c + 1],
                                      in_=PA[0:1, 0:T], axis=AX.X, op=ALU.max)
                tk = P.tick(ins, "vector")
                t_c0.append(tk)
                sc["psA_free"] = tk
            P.sec("vector", _)

    def _(v):
        P.wait(v, "vector", t_c0)
        ins = v.tensor_tensor(out=r8[0:1, 0:8], in0=r8[0:1, 0:8], in1=r8[0:1, 8:16],
                              op=ALU.mult)
        t_c0.append(P.tick(ins, "vector"))
    P.sec("vector", _)

    def _(s):
        P.wait(s, "scalar", t_c0)
        ins = s.activation(out=r8[0:1, 0:8], in_=r8[0:1, 0:8], func=AF.Sqrt)
        t_c0.append(P.tick(ins, "scalar"))
    P.sec("scalar", _)

    def _(v):
        P.wait(v, "vector", t_c0)
        ins = v.tensor_scalar(out=negb8[:], in0=r8[0:1, 0:8], scalar1=-scale,
                              scalar2=None, op0=ALU.mult, op1=ALU.bypass)
        t_c0.append(P.tick(ins, "vector"))
    P.sec("vector", _)

    def _(t):
        P.wait(t, "tensor", t_c0, sc.get("psB_guard"))
        ins = t.matmul(PB[:, 0:8], lhsT=cs["ones_col_bf"][:], rhs=negb8[:],
                       start=True, stop=True)
        t_c0.append(P.tick(ins, "tensor"))
    P.sec("tensor", _)

    def _(s):
        P.wait(s, "scalar", t_c0[-1])
        ins = s.activation(out=negc08[:], in_=PB[:, 0:8], func=AF.Copy)
        tk = P.tick(ins, "scalar")
        t_c0.append(tk)
        sc["psB_guard"] = tk
    P.sec("scalar", _)
    return negc08, t_c0[-1]


def emit_negc0(P, cs, sc, q_s, k_s, scale, tag):
    PA = sc["PA"]
    nsq = sc["t_bf"]
    negc0col = P.sb(tag + "_nc0", [128, 1], F32)
    r1 = P.sb(tag + "_nr1", [1, 2], F32)
    negc0b = P.sb(tag + "_nb", [1, 1], BF16)
    t_c0 = []
    nsq_g = [sc.get("g_tbf")]
    for wi, srcb in enumerate((q_s, k_s)):
        t_nmm = []
        for c in range(NCH):
            t_sq2 = []

            def _(s, srcb=srcb, c=c):
                P.wait(s, "scalar", t_c0, nsq_g[0])
                ins = s.activation(out=nsq[:, 0:T], in_=srcb[:, c * T:(c + 1) * T],
                                   func=AF.Square)
                t_sq2.append(P.tick(ins, "scalar"))
            P.sec("scalar", _)

            def _(t, c=c, t_sq2=t_sq2):
                P.wait(t, "tensor", t_sq2[-1], sc.get("psA_free"), t_c0)
                for j in range(NT512):
                    ins = t.matmul(PA[0:1, j * 512:(j + 1) * 512],
                                   lhsT=cs["ones_128"][:],
                                   rhs=nsq[:, j * 512:(j + 1) * 512],
                                   start=(c == 0), stop=(c == NCH - 1))
                tk = P.tick(ins, "tensor")
                nsq_g[0] = tk
                if c == NCH - 1:
                    t_nmm.append(tk)
            P.sec("tensor", _)

        def _(v, wi=wi, t_nmm=t_nmm):
            P.wait(v, "vector", t_nmm)
            ins = v.tensor_reduce(out=r1[0:1, wi:wi + 1], in_=PA[0:1, 0:T],
                                  axis=AX.X, op=ALU.max)
            t_c0.append(P.tick(ins, "vector"))
        P.sec("vector", _)

    def _(s):
        P.wait(s, "scalar", t_c0)
        ins = s.activation(out=r1[:], in_=r1[:], func=AF.Sqrt)
        t_c0.append(P.tick(ins, "scalar"))
    P.sec("scalar", _)

    def _(v):
        P.wait(v, "vector", t_c0)
        ins = v.tensor_tensor(out=r1[0:1, 0:1], in0=r1[0:1, 0:1], in1=r1[0:1, 1:2],
                              op=ALU.mult)
        P.chain(v, "vector", ins)
        ins = v.tensor_scalar(out=negc0b[:], in0=r1[0:1, 0:1], scalar1=-scale,
                              scalar2=None, op0=ALU.mult, op1=ALU.bypass)
        t_c0.append(P.tick(ins, "vector"))
    P.sec("vector", _)
    sc["psA_free"] = t_c0[-1]
    sc["g_tbf"] = nsq_g[0]

    def _(t):
        P.wait(t, "tensor", t_c0, sc.get("psB_guard"))
        ins = t.matmul(sc["PB"][:, 0:1], lhsT=cs["ones_col_bf"][:], rhs=negc0b[:],
                       start=True, stop=True)
        t_c0.append(P.tick(ins, "tensor"))
    P.sec("tensor", _)

    def _(s):
        P.wait(s, "scalar", t_c0[-1])
        ins = s.activation(out=negc0col[:], in_=sc["PB"][:, 0:1], func=AF.Copy)
        tk = P.tick(ins, "scalar")
        t_c0.append(tk)
        sc["psB_guard"] = tk
    P.sec("scalar", _)
    return negc0col, t_c0[-1]


def emit_qkv(P, cs, sc, w_s, wtok, xn_toks, os_, nm, evac_toks):
    PA = sc["PA"]
    xn = sc["xn"]
    for m in range(NCH):
        for j in range(NT512):
            bidx = (m * NT512 + j) % 4
            bank = PA[:, bidx * 512:(bidx + 1) * 512]
            mm_tok = []

            def _(t, m=m, j=j, bank=bank, bidx=bidx):
                P.wait(t, "tensor", wtok, xn_toks, sc.get("psA_free"),
                       evac_toks.get((nm, bidx)))
                for c in range(NCH):
                    ins = t.matmul(bank,
                                   lhsT=w_s[:, c * D + m * 128: c * D + (m + 1) * 128],
                                   rhs=xn[:, c * T + j * 512: c * T + (j + 1) * 512],
                                   start=(c == 0), stop=(c == NCH - 1))
                mm_tok.append(P.tick(ins, "tensor"))
            P.sec("tensor", _)

            def _(s, m=m, j=j, bank=bank, mm_tok=mm_tok, bidx=bidx):
                P.wait(s, "scalar", mm_tok[-1])
                ins = s.activation(out=os_[:, m * T + j * 512: m * T + (j + 1) * 512],
                                   in_=bank, func=AF.Copy)
                evac_toks[(nm, bidx)] = P.tick(ins, "scalar")
            P.sec("scalar", _)
    return [evac_toks[(nm, i)] for i in range(4)]


def emit_attn_a(P, cs, sc, W):
    xn = sc["xn"]
    PA = sc["PA"]
    PB = sc["PB"]

    xn_toks = emit_ln(P, cs, sc, "lna")

    w_s = P.sb("wa_s", [128, NCH * D], BF16)
    q_s = P.sb("qa_s", [128, NCH * T], BF16)
    k_s = P.sb("ka_s", [128, NCH * T], BF16)
    vt_s = P.sb("vta_s", [128, NTC * D], BF16)
    evac_toks = {}
    w_guard = [None]

    def load_w(src):
        wt = []

        def _(g):
            wt.append(P.dma(g, "gpsimd", w_s[:].rearrange("p (c n) -> p c n", c=NCH),
                            src.rearrange("(c p) n -> p c n", c=NCH),
                            waits=(w_guard[0],)))
        P.sec("gpsimd", _)
        return wt[-1]

    for (key, os_, nm) in (("wqT", q_s, "q"), ("wkT", k_s, "k")):
        wtok = load_w(W[key])
        emit_qkv(P, cs, sc, w_s, wtok, xn_toks, os_, nm, evac_toks)
        w_guard[0] = P.tok("tensor")
    sc["psA_free"] = [evac_toks[("q", i)] for i in range(4)] + \
                     [evac_toks[("k", i)] for i in range(4)]

    wtok = load_w(W["wvT"])
    for tc in range(NTC):
        for half in range(2):
            bidx = (tc * 2 + half) % 4
            bank = PA[:, bidx * 512:(bidx + 1) * 512]
            mm_tok = []

            def _(t, tc=tc, half=half, bank=bank, bidx=bidx, wtok=wtok):
                P.wait(t, "tensor", wtok, xn_toks, evac_toks.get(("v", bidx)),
                       sc.get("psA_free"))
                for c in range(NCH):
                    ins = t.matmul(bank,
                                   lhsT=xn[:, c * T + tc * 128: c * T + (tc + 1) * 128],
                                   rhs=w_s[:, c * D + half * 512: c * D + (half + 1) * 512],
                                   start=(c == 0), stop=(c == NCH - 1))
                mm_tok.append(P.tick(ins, "tensor"))
            P.sec("tensor", _)

            def _(s, tc=tc, half=half, bank=bank, mm_tok=mm_tok, bidx=bidx):
                P.wait(s, "scalar", mm_tok[-1])
                ins = s.activation(out=vt_s[:, tc * D + half * 512: tc * D + (half + 1) * 512],
                                   in_=bank, func=AF.Copy)
                evac_toks[("v", bidx)] = P.tick(ins, "scalar")
            P.sec("scalar", _)
    sc["psA_free"] = [evac_toks[("v", i)] for i in range(4)]
    w_guard[0] = P.tok("tensor")

    rope_toks = []
    for src in (q_s, k_s):
        for c in range(NCH):
            for j in range(NT512):
                bidx = j % 2
                bank = PA[:, bidx * 512:(bidx + 1) * 512]
                mm_tok = []
                sl = slice(c * T + j * 512, c * T + (j + 1) * 512)
                tsl = slice(j * 512, (j + 1) * 512)

                def _(t, src=src, c=c, j=j, bank=bank, sl=sl):
                    P.wait(t, "tensor", sc.get("psA_free"), rope_toks[-1] if rope_toks else None)
                    ins = t.matmul(bank, lhsT=cs["ropeP"][:], rhs=src[:, sl],
                                   start=True, stop=True)
                    mm_tok.append(P.tick(ins, "tensor"))
                P.sec("tensor", _)

                def _(v, src=src, c=c, j=j, bank=bank, mm_tok=mm_tok, sl=sl, tsl=tsl):
                    P.wait(v, "vector", mm_tok[-1])
                    tmp = sc["t_f32"]
                    ins = v.tensor_tensor(out=tmp[:, 0:512], in0=src[:, sl],
                                          in1=cs["rope_cos"][:, tsl], op=ALU.mult)
                    P.chain(v, "vector", ins)
                    ins = v.tensor_tensor(out=tmp[:, 512:1024], in0=bank,
                                          in1=cs["rope_sin"][:, tsl], op=ALU.mult)
                    P.chain(v, "vector", ins)
                    ins = v.tensor_tensor(out=src[:, sl], in0=tmp[:, 0:512],
                                          in1=tmp[:, 512:1024], op=ALU.add)
                    rope_toks.append(P.tick(ins, "vector"))
                P.sec("vector", _)
    sc["psA_free"] = rope_toks[-1]

    P.dbg_dump("qa", q_s[:], rope_toks)
    P.dbg_dump("ka", k_s[:], rope_toks)
    P.dbg_dump("vta", vt_s[:], [sc.get("psA_free")])
    negc08, c0_tok = emit_negc0_perchunk(P, cs, sc, q_s, k_s, 64.0 ** -0.5, "a")
    P.dbg_dump("nc0a", negc08[:], [c0_tok])

    ybuf = xn
    pts = P.sb("ptsa", [128, 6 * 128], BF16)
    qmt = P.sb("qmt", [128, 2 * 128], BF16)
    r2 = P.sb("r2a", [1, 256], F32)
    r2b = P.sb("r2ba", [1, 256], BF16)
    rb128 = P.sb("rb128", [128, 128], F32)
    y_ev = {"all": None}
    qm_toks = {}
    for qt in range(NTC):
        kts = list(range(max(0, qt - 2), qt + 1))
        for hp in range(NCH):
            exp_toks = {}
            mask_toks = {}
            av_tok = []
            den_tok = []
            for par in range(2):
                mcol = cs["maskeven"] if par == 0 else cs["maskodd"]

                def _(v, qt=qt, hp=hp, par=par, mcol=mcol):
                    P.wait(v, "vector", qm_toks.get(par), y_ev.get("all"))
                    ins = v.tensor_scalar(out=qmt[:, par * 128:(par + 1) * 128],
                                          in0=q_s[:, hp * T + qt * 128: hp * T + (qt + 1) * 128],
                                          scalar1=mcol[:, 0:1], scalar2=None,
                                          op0=ALU.mult, op1=ALU.bypass)
                    qm_toks[par] = P.tick(ins, "vector")
                P.sec("vector", _)

                for ki, kt in enumerate(kts):
                    sidx = 1 + (ki % 2)
                    pS = PA[:, sidx * 512: sidx * 512 + 128]
                    pslot = slice((ki * 2 + par) * 128, (ki * 2 + par + 1) * 128)
                    st_tok = []

                    def _(t, qt=qt, kt=kt, pS=pS, sidx=sidx, hp=hp, par=par):
                        P.wait(t, "tensor", c0_tok, exp_toks.get(sidx),
                               qm_toks[par], y_ev.get("all"))
                        ins = t.matmul(pS,
                                       lhsT=k_s[:, hp * T + kt * 128: hp * T + (kt + 1) * 128],
                                       rhs=qmt[:, par * 128:(par + 1) * 128],
                                       start=True, stop=True)
                        st_tok.append(P.tick(ins, "tensor"))
                    P.sec("tensor", _)

                    def _(s, pS=pS, sidx=sidx, pslot=pslot, st_tok=st_tok, hp=hp):
                        P.wait(s, "scalar", st_tok[-1])
                        ins = s.activation(out=pts[:, pslot], in_=pS, func=AF.Exp,
                                           scale=64.0 ** -0.5, bias=negc08[:, hp:hp + 1])
                        exp_toks[sidx] = P.tick(ins, "scalar")
                    P.sec("scalar", _)

                    mk = None
                    if kt == qt:
                        mk = "upper01"
                    elif kt == qt - 2:
                        mk = "supper01"
                    if mk:
                        def _(v, pslot=pslot, sidx=sidx, mk=mk, key=(par, ki)):
                            P.wait(v, "vector", exp_toks[sidx])
                            ins = v.tensor_tensor(out=pts[:, pslot], in0=pts[:, pslot],
                                                  in1=cs[mk][:], op=ALU.mult)
                            mask_toks[key] = P.tick(ins, "vector")
                        P.sec("vector", _)

                    def _(t, qt=qt, kt=kt, ki=ki, sidx=sidx, pslot=pslot, par=par,
                          hp=hp, key=(par, ki), nk=len(kts)):
                        P.wait(t, "tensor", exp_toks[sidx], mask_toks.get(key))
                        pt = pts[:, pslot]
                        ins = t.matmul(PB[0:1, par * 128:(par + 1) * 128],
                                       lhsT=cs["ones_128"][:], rhs=pt,
                                       start=(ki == 0 and par == 0),
                                       stop=(ki == nk - 1))
                        if ki == nk - 1:
                            den_tok.append(P.tick(ins, "tensor"))
                        ins = t.matmul(PA[:, par * 128:(par + 1) * 128],
                                       lhsT=vt_s[:, kt * D + hp * 128: kt * D + (hp + 1) * 128],
                                       rhs=pt, start=(ki == 0 and par == 0), stop=False)
                        if ki == nk - 1:
                            av_tok.append(P.tick(ins, "tensor"))
                    P.sec("tensor", _)

            t_den = []

            def _(s, av_tok=av_tok, den_tok=den_tok):
                P.wait(s, "scalar", den_tok, av_tok)
                ins = s.activation(out=r2[:], in_=PB[0:1, 0:256], func=AF.Copy)
                t_den.append(P.tick(ins, "scalar"))
            P.sec("scalar", _)

            def _(v, t_den=t_den):
                P.wait(v, "vector", t_den[-1])
                ins = v.reciprocal(out=r2[:], in_=r2[:])
                P.chain(v, "vector", ins)
                ins = v.tensor_scalar(out=r2b[:], in0=r2[:], scalar1=0.0,
                                      scalar2=None, op0=ALU.add, op1=ALU.bypass)
                t_den.append(P.tick(ins, "vector"))
            P.sec("vector", _)

            t_rb = []

            def _(t, t_den=t_den):
                P.wait(t, "tensor", t_den[-1], sc.get("psB_guard"))
                t.matmul(PB[:, 512:640], lhsT=cs["half0"][:], rhs=r2b[0:1, 0:128],
                         start=True, stop=False)
                ins = t.matmul(PB[:, 512:640], lhsT=cs["half1"][:], rhs=r2b[0:1, 128:256],
                               start=False, stop=True)
                t_rb.append(P.tick(ins, "tensor"))
            P.sec("tensor", _)

            def _(s, t_rb=t_rb):
                P.wait(s, "scalar", t_rb[-1])
                ins = s.activation(out=rb128[:], in_=PB[:, 512:640], func=AF.Copy)
                tk = P.tick(ins, "scalar")
                t_rb.append(tk)
                sc["psB_guard"] = tk
            P.sec("scalar", _)

            def _(v, qt=qt, hp=hp, av_tok=av_tok, t_rb=t_rb):
                P.wait(v, "vector", t_rb[-1], av_tok)
                sl = slice(hp * T + qt * 128, hp * T + (qt + 1) * 128)
                v.tensor_tensor(out=ybuf[0:64, sl], in0=PA[0:64, 0:128],
                                in1=rb128[0:64, :], op=ALU.mult)
                ins = v.tensor_tensor(out=ybuf[64:128, sl], in0=PA[64:128, 128:256],
                                      in1=rb128[64:128, :], op=ALU.mult)
                y_ev["all"] = P.tick(ins, "vector")
            P.sec("vector", _)

    wtok = load_w(W["woT"])
    return emit_proj_residual(P, cs, sc, w_s, ybuf, wtok, None, None)


def build_attn_a(prefix):
    wn = {"wqT": (D, D), "wkT": (D, D), "wvT": (D, D), "woT": (D, D),
          "_consts": LN_CONSTS + ["upper01", "supper01", "ropeP", "rope_cos",
                                  "rope_sin", "half0", "half1", "maskeven",
                                  "maskodd"]}
    nc, P, cs, sc, W = _base_prog(wn)
    res = emit_attn_a(P, cs, sc, W)
    _finish_prog(P, res, sc, sc.get("dbg") if DEBUG else None)

    PERM64 = np.concatenate([np.arange(0, 64, 2), np.arange(1, 64, 2)])
    PERM = np.concatenate([h * 64 + PERM64 for h in range(16)])

    def tq(a):
        return np.ascontiguousarray(np.asarray(a, np.float32)[PERM, :].T)

    def tt(a):
        return np.ascontiguousarray(np.asarray(a, np.float32).T)

    names = {"wqT": (prefix + "_Wq", tq), "wkT": (prefix + "_Wk", tq),
             "wvT": (prefix + "_Wv", tt), "woT": (prefix + "_Wo", tt)}
    return nc, names



def emit_mamba(P, cs, sc, W):
    xn = sc["xn"]
    PA = sc["PA"]
    PB = sc["PB"]
    PTb = sc["PTb"]
    stage = sc["stage"]
    NI = 16

    xn_toks = emit_ln(P, cs, sc, "lnm")

    xc_s = P.sb("xc_s", [128, NI * T], BF16)
    wbig = P.sb("wbigm", [128, NCH * T], BF16)
    xp = P.sb("xp_s", [128, NI * 32], BF16)
    dtT = P.sb("dtT_s", [16, T], BF16)
    cw = P.sb("cw_s", [128, 64], F32)
    cb = P.sb("cb_s", [128, NI], F32)
    dtb = P.sb("dtb_s", [128, NI], F32)
    Dp = P.sb("Dp_s", [128, NI], F32)
    B_s = P.sb("B_s", [16, T], BF16)
    spbuf = P.sb("spbuf", [128, 512], BF16)
    la_row = P.sb("la_row", [1, T], F32)
    xcT_s = P.sb("xcT_s", [128, NI * 128], BF16)
    Lt_s = P.sb("Lt_s", [128, 128], BF16)
    la_col = P.sb("la_col", [128, 1], F32)
    oma_col = P.sb("oma_col", [128, 1], F32)
    crow_s = P.sb("crow_s", [1, 128], F32)
    expc_row = P.sb("expc_row", [1, 128], BF16)
    hrow = [P.sb("hrowA", [1, T], BF16), None]
    R0, R1 = sc["R0"], sc["R1"]

    hrow[1] = B_s[0:1, 0:T]
    wt = []

    def _(g):
        wt.append(P.dma(g, "gpsimd", wbig[:].rearrange("p (c n) -> p c n", c=NCH),
                        W["winT"][:, 0:2048].rearrange("(c p) n -> p c n", c=NCH)))
        wt.append(P.dma(g, "gpsimd", xp[:].rearrange("p (c n) -> p c n", c=NI),
                        W["xprojT"].rearrange("(c p) n -> p c n", c=NI)))
        wt.append(P.dma(g, "gpsimd", dtT[:], W["dtT"]))
        wt.append(P.dma(g, "gpsimd", cw[:], W["convw"]))
        wt.append(P.dma(g, "gpsimd", cb[:], W["convb"]))
        wt.append(P.dma(g, "gpsimd", dtb[:], W["dtb"]))
        wt.append(P.dma(g, "gpsimd", Dp[:], W["Dp"]))
    P.sec("gpsimd", _)
    wtok = wt[-1]

    conv_toks = []
    for cc in range(NI):
        mm = []

        def _(t, cc=cc):
            P.wait(t, "tensor", wtok, xn_toks, conv_toks[-1] if conv_toks else None,
                   sc.get("psA_free"))
            for j in range(NT512):
                for c in range(NCH):
                    ins = t.matmul(PA[:, j * 512:(j + 1) * 512],
                                   lhsT=wbig[:, c * T + cc * 128: c * T + (cc + 1) * 128],
                                   rhs=xn[:, c * T + j * 512: c * T + (j + 1) * 512],
                                   start=(c == 0), stop=(c == NCH - 1))
            mm.append(P.tick(ins, "tensor"))
        P.sec("tensor", _)

        def _(v, cc=cc, mm=mm):
            P.wait(v, "vector", mm[-1])
            ins = v.tensor_scalar(out=stage[:, 0:T], in0=PA[:, 0:T],
                                  scalar1=cw[:, cc * 4 + 3: cc * 4 + 4], scalar2=None,
                                  op0=ALU.mult, op1=ALU.bypass)
            P.chain(v, "vector", ins)
            for k, off in ((2, 1), (1, 2), (0, 3)):
                ins = v.scalar_tensor_tensor(out=stage[:, off:T], in0=PA[:, 0:T - off],
                                             scalar=cw[:, cc * 4 + k: cc * 4 + k + 1],
                                             in1=stage[:, off:T],
                                             op0=ALU.mult, op1=ALU.add)
                P.chain(v, "vector", ins)
            conv_toks.append(P.tok("vector"))
        P.sec("vector", _)

        def _(s, cc=cc):
            P.wait(s, "scalar", conv_toks[-1])
            ins = s.activation(out=xc_s[:, cc * T:(cc + 1) * T], in_=stage[:, 0:T],
                               func=AF.Silu, bias=cb[:, cc:cc + 1])
            conv_toks.append(P.tick(ins, "scalar"))
        P.sec("scalar", _)

    bs_toks = []
    for j in range(NT512):
        mm = []

        def _(t, j=j):
            P.wait(t, "tensor", conv_toks[-1], bs_toks[-1] if bs_toks else None)
            for cc in range(NI):
                ins = t.matmul(PB[0:32, 0:512],
                               lhsT=xp[:, cc * 32:(cc + 1) * 32],
                               rhs=xc_s[:, cc * T + j * 512: cc * T + (j + 1) * 512],
                               start=(cc == 0), stop=(cc == NI - 1))
            mm.append(P.tick(ins, "tensor"))
        P.sec("tensor", _)

        def _(s, j=j, mm=mm):
            P.wait(s, "scalar", mm[-1])
            ins = s.activation(out=B_s[0:16, j * 512:(j + 1) * 512],
                               in_=PB[0:16, 0:512], func=AF.Copy)
            bs_toks.append(P.tick(ins, "scalar"))
        P.sec("scalar", _)

    al_toks = []
    for j in range(NT512):
        for dc in range(NI):
            mm = []

            def _(t, j=j, dc=dc):
                P.wait(t, "tensor", bs_toks, al_toks[-1] if al_toks else None)
                ins = t.matmul(PA[:, 0:512], lhsT=dtT[0:16, dc * 128:(dc + 1) * 128],
                               rhs=B_s[0:16, j * 512:(j + 1) * 512],
                               start=True, stop=True)
                mm.append(P.tick(ins, "tensor"))
            P.sec("tensor", _)
            t_sp = []

            def _(s, j=j, dc=dc, mm=mm):
                P.wait(s, "scalar", mm[-1])
                tf = sc["t_f32"]
                tb = sc["t_bf"]
                ins = s.activation(out=tf[:, 0:512], in_=PA[:, 0:512], func=AF.Relu,
                                   bias=dtb[:, dc:dc + 1])
                P.chain(s, "scalar", ins)
                ins = s.activation(out=tf[:, 512:1024], in_=PA[:, 0:512], func=AF.Abs,
                                   bias=dtb[:, dc:dc + 1])
                P.chain(s, "scalar", ins)
                ins = s.activation(out=tb[:, 0:512], in_=tf[:, 512:1024], func=AF.Exp,
                                   scale=-1.0)
                P.chain(s, "scalar", ins)
                ins = s.activation(out=tb[:, 512:1024], in_=tb[:, 0:512], func=AF.Ln,
                                   bias=1.0)
                t_sp.append(P.tick(ins, "scalar"))
            P.sec("scalar", _)

            def _(v, j=j, dc=dc, t_sp=t_sp):
                P.wait(v, "vector", t_sp[-1])
                tf = sc["t_f32"]
                tb = sc["t_bf"]
                ins = v.tensor_tensor(out=spbuf[:], in0=tf[:, 0:512],
                                      in1=tb[:, 512:1024], op=ALU.add)
                al_toks.append(P.tick(ins, "vector"))
            P.sec("vector", _)
            mm2 = []

            def _(t, j=j, dc=dc):
                P.wait(t, "tensor", al_toks[-1])
                ins = t.matmul(PB[0:1, 512:1024], lhsT=cs["ones_128"][:], rhs=spbuf[:],
                               start=(dc == 0), stop=(dc == NI - 1))
                mm2.append(P.tick(ins, "tensor"))
            P.sec("tensor", _)

        def _(v, j=j, mm2=mm2):
            P.wait(v, "vector", mm2[-1])
            ins = v.tensor_scalar(out=R0[0:1, j * 512:(j + 1) * 512],
                                  in0=PB[0:1, 512:1024], scalar1=1.0 / 2048.0,
                                  scalar2=None, op0=ALU.mult, op1=ALU.bypass)
            al_toks.append(P.tick(ins, "vector"))
        P.sec("vector", _)

    row_toks = []

    tmp_row = stage[0:1, 0:T]

    def _(s):
        P.wait(s, "scalar", al_toks)
        ins = s.activation(out=tmp_row, in_=R0[0:1, 0:T], func=AF.Relu, scale=-1.0)
        P.chain(s, "scalar", ins)
        ins = s.activation(out=la_row[:], in_=R0[0:1, 0:T], func=AF.Abs)
        P.chain(s, "scalar", ins)
        ins = s.activation(out=la_row[:], in_=la_row[:], func=AF.Exp, scale=-1.0)
        P.chain(s, "scalar", ins)
        ins = s.activation(out=la_row[:], in_=la_row[:], func=AF.Ln, bias=1.0)
        P.chain(s, "scalar", ins)
        ins = s.activation(out=R1[0:1, 0:T], in_=R0[0:1, 0:T], func=AF.Sigmoid)
        row_toks.append(P.tick(ins, "scalar"))
    P.sec("scalar", _)

    def _(v):
        P.wait(v, "vector", row_toks)
        ins = v.tensor_tensor(out=la_row[:], in0=la_row[:], in1=tmp_row, op=ALU.add)
        P.chain(v, "vector", ins)
        ins = v.tensor_scalar(out=la_row[:], in0=la_row[:], scalar1=-1.0, scalar2=None,
                              op0=ALU.mult, op1=ALU.bypass)
        P.chain(v, "vector", ins)
        ins = v.tensor_scalar(out=R1[0:1, 0:T], in0=R1[0:1, 0:T], scalar1=-1.0,
                              scalar2=1.0, op0=ALU.mult, op1=ALU.add)
        row_toks.append(P.tick(ins, "vector"))
    P.sec("vector", _)

    wt2 = []

    def _(g):
        wt2.append(P.dma(g, "gpsimd", wbig[:].rearrange("p (c n) -> p c n", c=NCH),
                         W["winT"][:, 2048:4096].rearrange("(c p) n -> p c n", c=NCH),
                         waits=(P.tok("tensor"), row_toks[-1])))
    P.sec("gpsimd", _)

    scan_toks = [None]
    for tc in range(NTC):
        tsl = slice(tc * 128, (tc + 1) * 128)
        tr = []

        def _(t, tc=tc, tsl=tsl):
            P.wait(t, "tensor", row_toks, scan_toks[-1], sc.get("psB_guard"))
            ins = t.transpose(PB[:, 512:513], in_=la_row[0:1, tsl],
                              identity=cs["ident"][0:1, 0:1])
            P.chain(t, "tensor", ins)
            ins = t.transpose(PB[:, 640:641], in_=R1[0:1, tsl],
                              identity=cs["ident"][0:1, 0:1])
            tr.append(P.tick(ins, "tensor"))
        P.sec("tensor", _)

        def _(s, tr=tr):
            P.wait(s, "scalar", tr[-1])
            ins = s.activation(out=la_col[:], in_=PB[:, 512:513], func=AF.Copy)
            P.chain(s, "scalar", ins)
            ins = s.activation(out=oma_col[:], in_=PB[:, 640:641], func=AF.Copy)
            tr.append(P.tick(ins, "scalar"))
        P.sec("scalar", _)

        def _(t, tc=tc, tr=tr):
            P.wait(t, "tensor", tr[-1])
            ins = t.matmul(PB[0:1, 512:640], lhsT=la_col[:], rhs=cs["triu_f32"][:],
                           start=True, stop=True)
            tr.append(P.tick(ins, "tensor"))
        P.sec("tensor", _)

        def _(s, tr=tr):
            P.wait(s, "scalar", tr[-1])
            ins = s.activation(out=crow_s[:], in_=PB[0:1, 512:640], func=AF.Copy)
            P.chain(s, "scalar", ins)
            ins = s.activation(out=expc_row[:], in_=PB[0:1, 512:640], func=AF.Exp)
            tr.append(P.tick(ins, "scalar"))
        P.sec("scalar", _)

        def _(t, tr=tr):
            P.wait(t, "tensor", tr[-1])
            t.matmul(PB[:, 640:768], lhsT=cs["ones_col"][:], rhs=crow_s[:],
                     start=True, stop=False)
            ins = t.matmul(PB[:, 640:768], lhsT=crow_s[:], rhs=cs["negones"][:],
                           start=False, stop=True)
            tr.append(P.tick(ins, "tensor"))
        P.sec("tensor", _)

        def _(v, tr=tr):
            P.wait(v, "vector", tr[-1])
            ins = v.tensor_tensor(out=sc["t_f32"][:, 0:128], in0=PB[:, 640:768],
                                  in1=cs["maskneg"][:], op=ALU.add)
            tr.append(P.tick(ins, "vector"))
        P.sec("vector", _)

        def _(s, tr=tr):
            P.wait(s, "scalar", tr[-1])
            ins = s.activation(out=Lt_s[:], in_=sc["t_f32"][:, 0:128], func=AF.Exp)
            tr.append(P.tick(ins, "scalar"))
        P.sec("scalar", _)

        xct_toks = []
        for cc in range(NI):
            def _(t, tc=tc, cc=cc, tr=tr):
                P.wait(t, "tensor", conv_toks, xct_toks[-1] if xct_toks else tr[-1])
                ins = t.transpose(PTb[:, 0:128],
                                  in_=xc_s[:, cc * T + tc * 128: cc * T + (tc + 1) * 128],
                                  identity=cs["ident_bf2"][:])
                xct_toks.append(P.tick(ins, "tensor"))
            P.sec("tensor", _)

            def _(v, cc=cc):
                P.wait(v, "vector", xct_toks[-1])
                ins = v.tensor_scalar(out=xcT_s[:, cc * 128:(cc + 1) * 128],
                                      in0=PTb[:, 0:128], scalar1=oma_col[:, 0:1],
                                      scalar2=None, op0=ALU.mult, op1=ALU.bypass)
                xct_toks.append(P.tick(ins, "vector"))
            P.sec("vector", _)

        mm3 = []

        def _(t, tc=tc, xct_toks=xct_toks, tr=tr):
            P.wait(t, "tensor", xct_toks[-1], tr[-1], scan_toks[-1])
            for cc in range(NI):
                ins = t.matmul(PA[:, cc * 128:(cc + 1) * 128],
                               lhsT=xcT_s[:, cc * 128:(cc + 1) * 128], rhs=Lt_s[:],
                               start=(cc % 4 == 0), stop=False)
                if tc > 0:
                    ins = t.matmul(PA[:, cc * 128:(cc + 1) * 128],
                                   lhsT=hrow[tc % 2][0:1, cc * 128:(cc + 1) * 128],
                                   rhs=expc_row[:], start=False, stop=False)
            mm3.append(P.tick(ins, "tensor"))
        P.sec("tensor", _)

        for j in range(4):
            mm4 = []

            def _(t, tc=tc, j=j, mm3=mm3):
                P.wait(t, "tensor", mm3[-1], sc.get("psB_guard"))
                ins = t.matmul(PB[0:1, 0:512], lhsT=Lt_s[:, 127:128],
                               rhs=xcT_s[:, j * 512:(j + 1) * 512],
                               start=True, stop=(tc == 0))
                if tc > 0:
                    ins = t.matmul(PB[0:1, 0:512], lhsT=expc_row[0:1, 127:128],
                                   rhs=hrow[tc % 2][0:1, j * 512:(j + 1) * 512],
                                   start=False, stop=True)
                mm4.append(P.tick(ins, "tensor"))
            P.sec("tensor", _)

            def _(s, tc=tc, j=j, mm4=mm4):
                P.wait(s, "scalar", mm4[-1])
                ins = s.activation(out=hrow[(tc + 1) % 2][0:1, j * 512:(j + 1) * 512],
                                   in_=PB[0:1, 0:512], func=AF.Copy)
                tk = P.tick(ins, "scalar")
                sc["psB_guard"] = tk
            P.sec("scalar", _)

        y_toks = []
        for cc in range(NI):
            mm5 = []

            def _(t, tc=tc, cc=cc, mm3=mm3):
                P.wait(t, "tensor", wt2[-1], mm3[-1], y_toks[-1] if y_toks else None)
                for c in range(NCH):
                    ins = t.matmul(PB[:, 1024:1152],
                                   lhsT=wbig[:, c * T + cc * 128: c * T + (cc + 1) * 128],
                                   rhs=xn[:, c * T + tc * 128: c * T + (tc + 1) * 128],
                                   start=(c == 0), stop=(c == NCH - 1))
                mm5.append(P.tick(ins, "tensor"))
            P.sec("tensor", _)

            def _(s, mm5=mm5):
                P.wait(s, "scalar", mm5[-1])
                ins = s.activation(out=sc["t_bf"][:, 0:128], in_=PB[:, 1024:1152],
                                   func=AF.Silu)
                mm5.append(P.tick(ins, "scalar"))
            P.sec("scalar", _)

            def _(v, tc=tc, cc=cc, mm5=mm5):
                P.wait(v, "vector", mm5[-1])
                sl = slice(cc * T + tc * 128, cc * T + (tc + 1) * 128)
                ins = v.scalar_tensor_tensor(out=sc["t_f32"][:, 0:128],
                                             in0=PA[:, cc * 128:(cc + 1) * 128],
                                             scalar=Dp[:, cc:cc + 1], in1=xc_s[:, sl],
                                             op0=ALU.mult, op1=ALU.add)
                P.chain(v, "vector", ins)
                ins = v.tensor_tensor(out=xc_s[:, sl], in0=sc["t_f32"][:, 0:128],
                                      in1=sc["t_bf"][:, 0:128], op=ALU.mult)
                y_toks.append(P.tick(ins, "vector"))
            P.sec("vector", _)
        scan_toks.append(y_toks[-1])

    wt3 = []

    def _(g):
        wt3.append(P.dma(g, "gpsimd", wbig[:].rearrange("p (c n) -> p c n", c=NTC),
                         W["outT"].rearrange("(c p) n -> p c n", c=NTC),
                         waits=(P.tok("tensor"), scan_toks[-1])))
    P.sec("gpsimd", _)
    return emit_proj_residual(P, cs, sc, wbig, xc_s, wt3[-1], None, None, nk=NTC)


def build_mamba(prefix):
    wn = {"winT": (D, 4096), "xprojT": (2048, 32), "dtT": (16, 2048),
          "outT": (2048, D), "convw": (128, 64), "convb": (128, 16),
          "dtb": (128, 16), "Dp": (128, 16),
          "_consts": LN_CONSTS + ["ident", "ident_bf2", "maskneg", "triu_f32",
                                  "ones_col", "negones"]}
    nc, P, cs, sc, W = _base_prog(wn)
    res = emit_mamba(P, cs, sc, W)
    _finish_prog(P, res, sc, sc.get("dbg") if DEBUG else None)

    def tt(a):
        return np.ascontiguousarray(np.asarray(a, np.float32).T)

    def cvw(a):
        a = np.asarray(a, np.float32)[:, 0, :]
        return np.ascontiguousarray(a.reshape(16, 128, 4).transpose(1, 0, 2)
                                    .reshape(128, 64))

    def col16(a):
        return np.ascontiguousarray(np.asarray(a, np.float32)
                                    .reshape(16, 128).T)

    names = {"winT": (prefix + "_in_proj_w", tt),
             "xprojT": (prefix + "_x_proj_w", tt),
             "dtT": (prefix + "_dt_proj_w", tt),
             "outT": (prefix + "_out_proj_w", tt),
             "convw": (prefix + "_conv_w", cvw),
             "convb": (prefix + "_conv_b", col16),
             "dtb": (prefix + "_dt_proj_b", col16),
             "Dp": (prefix + "_D_param", col16)}
    return nc, names



def emit_cmem(P, cs, sc, W, Wd):
    xn = sc["xn"]
    PA = sc["PA"]
    PB = sc["PB"]
    PTb = sc["PTb"]
    stage = sc["stage"]
    N2PI = -2.0 * math.pi
    INV2PI = 1.0 / (2.0 * math.pi)
    HPI = math.pi / 2.0

    xn_toks = emit_ln(P, cs, sc, "lnc")

    thetaB = P.sb("thetaB_s", [128, 8192], F32)
    ybuf = P.sb("ybuf_tm", [128, NTC * 1024], BF16)
    winb = ybuf[:, 0:NCH * 1056]
    wqs = P.sb("wqs", [128, T], BF16)
    kvcols = P.sb("kvcols", [128, 1024], F32)
    kvT = P.sb("kvT", [128, 1024], F32)
    pwT = P.sb("pwT", [128, 32], F32)
    pwTb = P.sb("pwTb", [128, 32], BF16)
    invk = P.sb("invk", [128, 32], F32)
    qr_p = P.sb("qr_p", [128, 1024], BF16)
    qi_p = P.sb("qi_p", [128, 1024], BF16)
    pbuf1 = P.sb("pbuf1", [128, 1024], F32)
    qn32 = P.sb("qn32", [128, 1024], I32)
    ypre = sc["t_f32"]
    sraw = P.sb("sraw", [32, T], F32)
    ck_s = P.sb("ck_s", [128, 36], F32)
    fr32 = P.sb("fr32", [1, 64], F32)
    srow = P.sb("srow", [1, 256], F32)
    frow = P.sb("frow", [1, 1024], BF16)
    freB = P.sb("freB", [128, 1024], BF16)
    fimB = P.sb("fimB", [128, 1024], BF16)
    l2 = P.sb("l2", [2, 32], F32)
    sscol = P.sb("sscol", [32, 1], F32)
    carryd = P.sb("carryd", [1, 32], BF16)
    Iq = [sc["abc_s"], sc["bbc_s"]]
    qv = [qr_p, qi_p]
    Rb0, Rb1 = sc["Rb0"], sc["Rb1"]
    kv_d = Wd["kv_d"]
    carry_d = Wd["carry_d"]
    l2_d = Wd["l2_d"]

    wt = []

    def _(g):
        wt.append(P.dma(g, "gpsimd", thetaB[:], W["thetaB"]))
        wt.append(P.dma(g, "gpsimd", winb.rearrange("p (c n) -> p c n", c=NCH),
                        W["win"][:, 0:1056].rearrange("(c p) n -> p c n", c=NCH)))
        wt.append(P.dma(g, "gpsimd", ck_s[:], W["convk"]))
        wt.append(P.dma(g, "gpsimd", fr32[:], W["nsc"]))
        wt.append(P.dma(g, "gpsimd", sscol[:], W["ssc"]))
        wt.append(P.dma(g, "gpsimd", srow[0:1, 0:32], W["slopes"]))
        wt.append(P.dma(g, "gpsimd", srow[0:1, 32:64], W["fre"]))
        wt.append(P.dma(g, "gpsimd", srow[0:1, 64:96], W["fim"]))
    P.sec("gpsimd", _)
    wtok = wt[-1]

    fb = []

    def _(v):
        P.wait(v, "vector", wtok)
        ins = v.tensor_tensor(out=srow[0:1, 32:64], in0=srow[0:1, 32:64],
                              in1=fr32[0:1, 0:32], op=ALU.mult)
        P.chain(v, "vector", ins)
        ins = v.tensor_tensor(out=srow[0:1, 64:96], in0=srow[0:1, 64:96],
                              in1=fr32[0:1, 32:64], op=ALU.mult)
        P.chain(v, "vector", ins)
        ins = v.tensor_copy(
            out=frow[0:1, 0:1024].rearrange("a (q h m) -> a q h m", q=4, h=32),
            in_=srow[0:1, 32:64].unsqueeze(1).unsqueeze(3).broadcast_to([1, 4, 32, 8]))
        fb.append(P.tick(ins, "vector"))
    P.sec("vector", _)

    def _(t):
        P.wait(t, "tensor", fb[-1], sc.get("psB_guard"))
        t.matmul(PB[:, 0:512], lhsT=cs["ones_col_bf"][:], rhs=frow[0:1, 0:512],
                 start=True, stop=True)
        ins = t.matmul(PB[:, 512:1024], lhsT=cs["ones_col_bf"][:],
                       rhs=frow[0:1, 512:1024], start=True, stop=True)
        fb.append(P.tick(ins, "tensor"))
    P.sec("tensor", _)

    def _(s):
        P.wait(s, "scalar", fb[-1])
        s.activation(out=freB[:, 0:512], in_=PB[:, 0:512], func=AF.Copy)
        ins = s.activation(out=freB[:, 512:1024], in_=PB[:, 512:1024], func=AF.Copy)
        fb.append(P.tick(ins, "scalar"))
    P.sec("scalar", _)

    def _(v):
        P.wait(v, "vector", fb[-1])
        ins = v.tensor_copy(
            out=frow[0:1, 0:1024].rearrange("a (q h m) -> a q h m", q=4, h=32),
            in_=srow[0:1, 64:96].unsqueeze(1).unsqueeze(3).broadcast_to([1, 4, 32, 8]))
        fb.append(P.tick(ins, "vector"))
    P.sec("vector", _)

    def _(t):
        P.wait(t, "tensor", fb[-1])
        t.matmul(PB[:, 0:512], lhsT=cs["ones_col_bf"][:], rhs=frow[0:1, 0:512],
                 start=True, stop=True)
        ins = t.matmul(PB[:, 512:1024], lhsT=cs["ones_col_bf"][:],
                       rhs=frow[0:1, 512:1024], start=True, stop=True)
        fb.append(P.tick(ins, "tensor"))
    P.sec("tensor", _)

    def _(s):
        P.wait(s, "scalar", fb[-1])
        s.activation(out=fimB[:, 0:512], in_=PB[:, 0:512], func=AF.Copy)
        ins = s.activation(out=fimB[:, 512:1024], in_=PB[:, 512:1024], func=AF.Copy)
        tk = P.tick(ins, "scalar")
        fb.append(tk)
        sc["psB_guard"] = tk
    P.sec("scalar", _)

    row_toks = []

    def _(s):
        P.wait(s, "scalar", wtok)
        ins = s.activation(out=srow[0:1, 96:128], in_=srow[0:1, 0:32], func=AF.Relu)
        P.chain(s, "scalar", ins)
        ins = s.activation(out=srow[0:1, 128:160], in_=srow[0:1, 0:32], func=AF.Abs)
        P.chain(s, "scalar", ins)
        ins = s.activation(out=srow[0:1, 128:160], in_=srow[0:1, 128:160],
                           func=AF.Exp, scale=-1.0)
        P.chain(s, "scalar", ins)
        ins = s.activation(out=srow[0:1, 128:160], in_=srow[0:1, 128:160],
                           func=AF.Ln, bias=1.0)
        row_toks.append(P.tick(ins, "scalar"))
    P.sec("scalar", _)

    def _(v):
        P.wait(v, "vector", row_toks[-1], fb[0])
        ins = v.tensor_tensor(out=srow[0:1, 160:192], in0=srow[0:1, 96:128],
                              in1=srow[0:1, 128:160], op=ALU.add)
        P.chain(v, "vector", ins)
        ins = v.tensor_tensor(out=srow[0:1, 192:224], in0=srow[0:1, 160:192],
                              in1=cs["dmask"][:], op=ALU.mult)
        P.chain(v, "vector", ins)
        ins = v.tensor_tensor(out=srow[0:1, 224:256], in0=srow[0:1, 160:192],
                              in1=cs["amask"][:], op=ALU.mult)
        row_toks.append(P.tick(ins, "vector"))
    P.sec("vector", _)

    def _(g):
        row_toks.append(P.dma(g, "gpsimd", l2_d.rearrange("p k -> (p k)").unsqueeze(0),
                              srow[0:1, 192:256], waits=(row_toks[-1],)))
        row_toks.append(P.dma(g, "gpsimd", l2[:], l2_d, waits=(row_toks[-1],)))
    P.sec("gpsimd", _)

    conv_toks = []
    for cc in range(9):
        mm = []
        mw = 128 if cc < 8 else 32

        def _(t, cc=cc, mw=mw, mm=mm):
            P.wait(t, "tensor", wt[1], wt[2], xn_toks,
                   conv_toks[-1] if conv_toks else None)
            for j in range(NT512):
                for c in range(NCH):
                    ins = t.matmul(PA[0:mw, j * 512:(j + 1) * 512],
                                   lhsT=winb[:, c * 1056 + cc * 128: c * 1056 + cc * 128 + mw],
                                   rhs=xn[:, c * T + j * 512: c * T + (j + 1) * 512],
                                   start=(c == 0), stop=(c == NCH - 1))
            mm.append(P.tick(ins, "tensor"))
        P.sec("tensor", _)

        dst = sraw if cc == 8 else stage

        def _(v, cc=cc, mw=mw, mm=mm, dst=dst):
            P.wait(v, "vector", mm[-1])
            ins = v.tensor_scalar(out=dst[0:mw, 0:T], in0=PA[0:mw, 0:T],
                                  scalar1=ck_s[0:mw, cc * 4 + 3: cc * 4 + 4],
                                  scalar2=None, op0=ALU.mult, op1=ALU.bypass)
            P.chain(v, "vector", ins)
            for k, off in ((2, 1), (1, 2), (0, 3)):
                ins = v.scalar_tensor_tensor(out=dst[0:mw, off:T],
                                             in0=PA[0:mw, 0:T - off],
                                             scalar=ck_s[0:mw, cc * 4 + k: cc * 4 + k + 1],
                                             in1=dst[0:mw, off:T],
                                             op0=ALU.mult, op1=ALU.add)
                P.chain(v, "vector", ins)
            conv_toks.append(P.tok("vector"))
        P.sec("vector", _)

        if cc < 8:
            def _(sy, cc=cc):
                conv_toks.append(P.dma(sy, "sync", kv_d[cc * 128:(cc + 1) * 128, :],
                                       stage[:, 0:T], waits=(conv_toks[-1],)))
            P.sec("sync", _)
    kv_done = conv_toks[-1]

    TR = globals().get("L1B_TRUNC", {})
    if not TR.get("pw", True):
        return [conv_toks[-1]]
    pw_toks = []

    def _(t):
        P.wait(t, "tensor", row_toks[-1], conv_toks[-1])
        for j in range(NT512):
            ins = t.matmul(PA[0:32, j * 512:(j + 1) * 512], lhsT=l2[:, 0:32],
                           rhs=cs["distpos"][:, j * 512:(j + 1) * 512],
                           start=True, stop=True)
        pw_toks.append(P.tick(ins, "tensor"))
    P.sec("tensor", _)

    def _(v):
        P.wait(v, "vector", pw_toks[-1], wtok)
        ins = v.tensor_scalar(out=sraw[:, 0:T], in0=sraw[:, 0:T],
                              scalar1=sscol[0:32, 0:1], scalar2=None,
                              op0=ALU.mult, op1=ALU.bypass)
        P.chain(v, "vector", ins)
        ins = v.tensor_scalar(out=sraw[:, 0:T], in0=sraw[:, 0:T], scalar1=20.0,
                              scalar2=None, op0=ALU.min, op1=ALU.bypass)
        P.chain(v, "vector", ins)
        ins = v.tensor_scalar(out=sraw[:, 0:T], in0=sraw[:, 0:T], scalar1=-20.0,
                              scalar2=None, op0=ALU.max, op1=ALU.bypass)
        P.chain(v, "vector", ins)
        ins = v.tensor_tensor(out=stage[0:32, 0:T], in0=sraw[:, 0:T],
                              in1=PA[0:32, 0:T], op=ALU.add)
        P.chain(v, "vector", ins)
        ins = v.tensor_scalar(out=stage[0:32, 0:T], in0=stage[0:32, 0:T],
                              scalar1=-87.0, scalar2=None, op0=ALU.max,
                              op1=ALU.bypass)
        pw_toks.append(P.tick(ins, "vector"))
    P.sec("vector", _)

    def _(s):
        P.wait(s, "scalar", pw_toks[-1])
        ins = s.activation(out=sraw[:, 0:T], in_=stage[0:32, 0:T], func=AF.Exp)
        pw_toks.append(P.tick(ins, "scalar"))
    P.sec("scalar", _)

    if TR.get("ntc", 16) == 0:
        return [pw_toks[-1]]
    wq_guard = [None]
    y_toks = [None]
    carry_st = {}
    for tc in range(min(NTC, TR.get("ntc", 16))):
        tsl = slice(tc * 128, (tc + 1) * 128)
        tr = []

        def _(sy, tc=tc, tsl=tsl):
            tr.append(P.dma(sy, "sync", kvcols[:].rearrange("p (c n) -> p c n", c=8),
                            kv_d.rearrange("(c p) t -> p c t", c=8)[:, :, tsl],
                            waits=(kv_done,)))
        P.sec("sync", _)

        for i in range(8):
            def _(t, i=i, tr=tr):
                P.wait(t, "tensor", tr[-1], y_toks[-1])
                ins = t.transpose(PB[:, 0:128], in_=kvcols[:, i * 128:(i + 1) * 128],
                                  identity=cs["ident"][:])
                tr.append(P.tick(ins, "tensor"))
            P.sec("tensor", _)

            def _(s, i=i, tr=tr):
                P.wait(s, "scalar", tr[-1])
                ins = s.activation(out=kvT[:, i * 128:(i + 1) * 128], in_=PB[:, 0:128],
                                   func=AF.Copy)
                tr.append(P.tick(ins, "scalar"))
            P.sec("scalar", _)

        def _(t, tc=tc, tsl=tsl, tr=tr):
            P.wait(t, "tensor", pw_toks[-1], tr[-1])
            ins = t.transpose(PB[:, 0:32], in_=sraw[0:32, tsl],
                              identity=cs["ident"][0:32, 0:32])
            tr.append(P.tick(ins, "tensor"))
        P.sec("tensor", _)

        def _(s, tr=tr):
            P.wait(s, "scalar", tr[-1])
            ins = s.activation(out=pwT[:], in_=PB[:, 0:32], func=AF.Copy)
            P.chain(s, "scalar", ins)
            ins = s.activation(out=pwTb[:], in_=PB[:, 0:32], func=AF.Copy)
            tr.append(P.tick(ins, "scalar"))
        P.sec("scalar", _)

        qmm = []
        for c in range(NCH):
            ld = []

            def _(g, c=c, ld=ld):
                ld.append(P.dma(g, "gpsimd", wqs[:],
                                W["win"][c * 128:(c + 1) * 128, 1056:3104],
                                waits=(wq_guard[0],)))
            P.sec("gpsimd", _)

            def _(t, c=c, tc=tc, ld=ld, qmm=qmm):
                P.wait(t, "tensor", ld[-1], y_toks[-1] if c == 0 else None)
                for n in range(4):
                    ins = t.matmul(PA[:, n * 512:(n + 1) * 512],
                                   lhsT=xn[:, c * T + tc * 128: c * T + tc * 128 + 128],
                                   rhs=wqs[:, n * 512:(n + 1) * 512],
                                   start=(c == 0), stop=(c == NCH - 1))
                tk = P.tick(ins, "tensor")
                qmm.append(tk)
                wq_guard[0] = tk
            P.sec("tensor", _)

        def _(s, qmm=qmm):
            P.wait(s, "scalar", qmm[-1])
            for j in range(4):
                ins = s.activation(out=stage[:, j * 512:(j + 1) * 512],
                                   in_=PA[:, j * 512:(j + 1) * 512], func=AF.Copy)
            qmm.append(P.tick(ins, "scalar"))
        P.sec("scalar", _)

        ex = []

        def _(v, qmm=qmm):
            P.wait(v, "vector", qmm[-1])
            qrv = stage[:, 0:2048].rearrange("p (n r) -> p n r", r=2)[:, :, 0:1]
            qiv = stage[:, 0:2048].rearrange("p (n r) -> p n r", r=2)[:, :, 1:2]
            t1 = ypre[:, 0:1024].unsqueeze(2)
            t2 = pbuf1[:, 0:1024].unsqueeze(2)
            ins = v.tensor_tensor(out=t1, in0=qrv, in1=freB[:, 0:1024].unsqueeze(2),
                                  op=ALU.mult)
            P.chain(v, "vector", ins)
            ins = v.tensor_tensor(out=t2, in0=qiv, in1=fimB[:, 0:1024].unsqueeze(2),
                                  op=ALU.mult)
            P.chain(v, "vector", ins)
            ins = v.tensor_tensor(out=qr_p[:], in0=ypre[:, 0:1024],
                                  in1=pbuf1[:, 0:1024], op=ALU.subtract)
            P.chain(v, "vector", ins)
            ins = v.tensor_tensor(out=t1, in0=qiv, in1=freB[:, 0:1024].unsqueeze(2),
                                  op=ALU.mult)
            P.chain(v, "vector", ins)
            ins = v.tensor_tensor(out=t2, in0=qrv, in1=fimB[:, 0:1024].unsqueeze(2),
                                  op=ALU.mult)
            P.chain(v, "vector", ins)
            ins = v.tensor_tensor(out=qi_p[:], in0=ypre[:, 0:1024],
                                  in1=pbuf1[:, 0:1024], op=ALU.add)
            ex.append(P.tick(ins, "vector"))
        P.sec("vector", _)

        for q4 in range(4):
            tg = []
            for hf in range(2):
                qcol = q4 * 2048 + hf * 1024
                base = q4 * 256 + hf * 128
                kk = q4 * 8 + hf * 4

                def _(v, qcol=qcol, base=base, tg=tg):
                    P.wait(v, "vector", tr[-1], ex[-1],
                           tg[-1] if tg else None)
                    ins = v.tensor_tensor(
                        out=pbuf1[:, 0:1024].rearrange("p (k m) -> p k m", m=8),
                        in0=kvT[:, base:base + 128].unsqueeze(2).broadcast_to([128, 128, 8]),
                        in1=thetaB[:, qcol:qcol + 1024].rearrange("p (k m) -> p k m", m=8),
                        op=ALU.mult)
                    tg.append(P.tick(ins, "vector"))
                P.sec("vector", _)

                def _(s, tg=tg):
                    P.wait(s, "scalar", tg[-1])
                    ins = s.activation(out=qn32[:, 0:1024], in_=pbuf1[:, 0:1024],
                                       func=AF.Copy, scale=INV2PI)
                    tg.append(P.tick(ins, "scalar"))
                P.sec("scalar", _)

                def _(v, tg=tg):
                    P.wait(v, "vector", tg[-1])
                    ins = v.scalar_tensor_tensor(out=stage[:, 0:1024],
                                                 in0=qn32[:, 0:1024], scalar=N2PI,
                                                 in1=pbuf1[:, 0:1024],
                                                 op0=ALU.mult, op1=ALU.add)
                    tg.append(P.tick(ins, "vector"))
                P.sec("vector", _)

                def _(s, tg=tg):
                    P.wait(s, "scalar", tg[-1])
                    ins = s.activation(out=pbuf1[:, 0:1024], in_=stage[:, 0:1024],
                                       func=AF.Sin)
                    tg.append(P.tick(ins, "scalar"))
                P.sec("scalar", _)

                def _(v, hf=hf, kk=kk, tg=tg):
                    P.wait(v, "vector", tg[-1])
                    ins = v.tensor_tensor(
                        out=Iq[1][:, hf * 1024:(hf + 1) * 1024].rearrange(
                            "p (k c) -> p k c", k=4),
                        in0=pbuf1[:, 0:1024].rearrange("p (k c) -> p k c", k=4),
                        in1=pwT[:, kk:kk + 4].unsqueeze(2).broadcast_to([128, 4, 256]),
                        op=ALU.mult)
                    tg.append(P.tick(ins, "vector"))
                P.sec("vector", _)

                def _(s, tg=tg):
                    P.wait(s, "scalar", tg[-1])
                    ins = s.activation(out=pbuf1[:, 0:1024], in_=stage[:, 0:1024],
                                       func=AF.Abs)
                    tg.append(P.tick(ins, "scalar"))
                P.sec("scalar", _)

                def _(v, tg=tg):
                    P.wait(v, "vector", tg[-1])
                    ins = v.tensor_scalar(out=stage[:, 0:1024], in0=pbuf1[:, 0:1024],
                                          scalar1=-1.0, scalar2=HPI,
                                          op0=ALU.mult, op1=ALU.add)
                    tg.append(P.tick(ins, "vector"))
                P.sec("vector", _)

                def _(s, tg=tg):
                    P.wait(s, "scalar", tg[-1])
                    ins = s.activation(out=pbuf1[:, 0:1024], in_=stage[:, 0:1024],
                                       func=AF.Sin)
                    tg.append(P.tick(ins, "scalar"))
                P.sec("scalar", _)

                def _(v, hf=hf, kk=kk, tg=tg):
                    P.wait(v, "vector", tg[-1])
                    ins = v.tensor_tensor(
                        out=Iq[0][:, hf * 1024:(hf + 1) * 1024].rearrange(
                            "p (k c) -> p k c", k=4),
                        in0=pbuf1[:, 0:1024].rearrange("p (k c) -> p k c", k=4),
                        in1=pwT[:, kk:kk + 4].unsqueeze(2).broadcast_to([128, 4, 256]),
                        op=ALU.mult)
                    tg.append(P.tick(ins, "vector"))
                P.sec("vector", _)

            for dom in range(2):
                crow = (dom * 4 + q4,)
                cl = []
                if tc > 0:
                    def _(sy, crow=crow, cl=cl):
                        cl.append(P.dma(sy, "sync", Rb0[0:1, 0:T],
                                        carry_d[crow[0]:crow[0] + 1, :],
                                        waits=(carry_st.get(crow[0]),)))
                    P.sec("sync", _)
                cn = []
                for j in range(4):
                    def _(t, dom=dom, j=j, tc=tc, tg=tg, cl=cl, cn=cn):
                        P.wait(t, "tensor", tg[-1], cl[-1] if cl else None,
                               cn[-1] if cn else None, qmm[-1])
                        jsl = slice(j * 512, (j + 1) * 512)
                        ins = t.matmul(PA[:, jsl], lhsT=cs["triu_bf"][:],
                                       rhs=Iq[dom][:, jsl],
                                       start=True, stop=(tc == 0))
                        if tc > 0:
                            ins = t.matmul(PA[:, jsl],
                                           lhsT=cs["ones_col_bf"][0:1, :],
                                           rhs=Rb0[0:1, jsl],
                                           start=False, stop=True)
                        ins2 = t.matmul(PB[0:1, 512:1024], lhsT=cs["ones_128"][:],
                                        rhs=Iq[dom][:, jsl],
                                        start=True, stop=(tc == 0))
                        if tc > 0:
                            ins2 = t.matmul(PB[0:1, 512:1024],
                                            lhsT=cs["ones_col_bf"][0:1, 0:1],
                                            rhs=Rb0[0:1, jsl],
                                            start=False, stop=True)
                        cn.append(P.tick(ins2, "tensor"))
                    P.sec("tensor", _)

                    def _(s, j=j, cn=cn):
                        P.wait(s, "scalar", cn[-1])
                        ins = s.activation(out=Rb1[0:1, j * 512:(j + 1) * 512],
                                           in_=PB[0:1, 512:1024], func=AF.Copy)
                        cn.append(P.tick(ins, "scalar"))
                    P.sec("scalar", _)

                def _(v, dom=dom, q4=q4, cn=cn):
                    P.wait(v, "vector", cn[-1])
                    for j in range(4):
                        jsl = slice(j * 512, (j + 1) * 512)
                        ins = v.tensor_tensor(
                            out=stage[:, 1024:1536].rearrange("p (k c) -> p k c", k=2),
                            in0=PA[:, jsl].rearrange("p (k c) -> p k c", k=2),
                            in1=qv[dom][:, q4 * 256:(q4 + 1) * 256]
                                .unsqueeze(1).broadcast_to([128, 2, 256]),
                            op=ALU.mult)
                        P.chain(v, "vector", ins)
                        ins = v.tensor_reduce(
                            out=pbuf1[:, dom * 256 + j * 64: dom * 256 + (j + 1) * 64],
                            in_=stage[:, 1024:1536].rearrange("p (n m) -> p n m", m=8),
                            axis=AX.X, op=ALU.add)
                        P.chain(v, "vector", ins)
                    cn.append(P.tok("vector"))
                P.sec("vector", _)

                def _(g, crow=crow, cn=cn):
                    tk = P.dma(g, "gpsimd", carry_d[crow[0]:crow[0] + 1, :],
                               Rb1[0:1, 0:T], waits=(cn[-1],))
                    carry_st[crow[0]] = tk
                P.sec("gpsimd", _)

            def _(v, q4=q4):
                P.wait(v, "vector", P.tok("vector"))
                ins = v.tensor_tensor(out=ypre[:, q4 * 256:(q4 + 1) * 256],
                                      in0=pbuf1[:, 0:256], in1=pbuf1[:, 256:512],
                                      op=ALU.add)
                P.chain(v, "vector", ins)
            P.sec("vector", _)

        dn = []

        def _(t, tc=tc, tr=tr):
            P.wait(t, "tensor", tr[-1])
            ins = t.matmul(PB[:, 1056:1088], lhsT=cs["triu_bf"][:], rhs=pwTb[:],
                           start=True, stop=(tc == 0))
            if tc > 0:
                ins = t.matmul(PB[:, 1056:1088], lhsT=cs["ones_col_bf"][0:1, :],
                               rhs=carryd[:], start=False, stop=True)
            dn.append(P.tick(ins, "tensor"))
        P.sec("tensor", _)

        def _(v, dn=dn):
            P.wait(v, "vector", dn[-1])
            ins = v.tensor_scalar(out=pbuf1[:, 0:32], in0=PB[:, 1056:1088],
                                  scalar1=1e-4, scalar2=None, op0=ALU.max,
                                  op1=ALU.bypass)
            P.chain(v, "vector", ins)
            ins = v.reciprocal(out=invk[:], in_=pbuf1[:, 0:32])
            dn.append(P.tick(ins, "vector"))
        P.sec("vector", _)

        def _(t, tc=tc, dn=dn):
            P.wait(t, "tensor", dn[-1])
            ins = t.matmul(PB[0:1, 1088:1120], lhsT=cs["ones_128"][:], rhs=pwTb[:],
                           start=True, stop=(tc == 0))
            if tc > 0:
                ins = t.matmul(PB[0:1, 1088:1120], lhsT=cs["ones_col_bf"][0:1, 0:1],
                               rhs=carryd[:], start=False, stop=True)
            dn.append(P.tick(ins, "tensor"))
        P.sec("tensor", _)

        def _(s, dn=dn):
            P.wait(s, "scalar", dn[-1])
            ins = s.activation(out=carryd[:], in_=PB[0:1, 1088:1120], func=AF.Copy)
            dn.append(P.tick(ins, "scalar"))
        P.sec("scalar", _)

        def _(v, dn=dn):
            P.wait(v, "vector", dn[-1])
            ins = v.tensor_tensor(
                out=stage[:, 0:1024].rearrange("p (k h) -> p k h", k=32),
                in0=ypre[:, 0:1024].rearrange("p (k h) -> p k h", k=32),
                in1=invk[:, 0:32].unsqueeze(2).broadcast_to([128, 32, 32]),
                op=ALU.mult)
            dn.append(P.tick(ins, "vector"))
        P.sec("vector", _)

        def _(s, tc=tc, dn=dn):
            P.wait(s, "scalar", dn[-1])
            ins = s.activation(out=ybuf[:, tc * 1024:(tc + 1) * 1024],
                               in_=stage[:, 0:1024], func=AF.Silu)
            tk = P.tick(ins, "scalar")
            dn.append(tk)
            y_toks.append(tk)
        P.sec("scalar", _)

    if not TR.get("tail", True):
        return [y_toks[-1]]
    tp = []
    for tc in range(NTC):
        for cc in range(NCH):
            def _(t, tc=tc, cc=cc):
                P.wait(t, "tensor", y_toks[-1], tp[-1] if tp else None)
                ins = t.transpose(PTb[:, 0:128],
                                  in_=ybuf[:, tc * 1024 + cc * 128: tc * 1024 + (cc + 1) * 128],
                                  identity=cs["ident_bf2"][:])
                tp.append(P.tick(ins, "tensor"))
            P.sec("tensor", _)

            def _(v, tc=tc, cc=cc):
                P.wait(v, "vector", tp[-1])
                ins = v.tensor_copy(out=xn[:, cc * T + tc * 128: cc * T + (tc + 1) * 128],
                                    in_=PTb[:, 0:128])
                tp.append(P.tick(ins, "vector"))
            P.sec("vector", _)

    wo = []

    def _(g):
        wo.append(P.dma(g, "gpsimd", ybuf[:, 0:8192].rearrange("p (c n) -> p c n", c=NCH),
                        W["wout"].rearrange("(c p) n -> p c n", c=NCH),
                        waits=(tp[-1],)))
    P.sec("gpsimd", _)

    sc["g_stage"] = tp[-1]
    return emit_proj_residual(P, cs, sc, ybuf[:, 0:8192], xn,
                              [wo[-1], tp[-1]], nk=NCH)


def build_l1b(prefix):
    wn = {"win": (D, 3104), "wout": (D, D), "convk": (128, 36),
          "thetaB": (128, 8192), "slopes": (1, 32), "ssc": (32, 1),
          "fre": (1, 32), "fim": (1, 32), "nsc": (1, 64),
          "_consts": LN_CONSTS + ["ident", "ident_bf2", "triu_bf",
                                  "dmask", "amask", "distpos"]}
    nc, P, cs, sc, W = _base_prog(wn)
    Wd = {
        "kv_d": nc.dram_tensor("kv_d", [D, T], F32, kind="Internal").ap(),
        "carry_d": nc.dram_tensor("carry_d", [8, T], BF16, kind="Internal").ap(),
        "l2_d": nc.dram_tensor("l2_d", [2, 32], F32, kind="Internal").ap(),
    }
    res = emit_cmem(P, cs, sc, W, Wd)
    _finish_prog(P, res, sc, sc.get("dbg") if DEBUG else None)

    def cvk(a):
        a = np.asarray(a, np.float32)[:, 0, :]
        out = np.zeros((128, 36), np.float32)
        for cc in range(9):
            w = min(128, 1056 - cc * 128)
            out[:w, cc * 4:(cc + 1) * 4] = a[:, cc * 128:cc * 128 + w].T
        return out

    def th(a):
        return np.ascontiguousarray(np.broadcast_to(
            np.asarray(a, np.float32).reshape(1, 8192), (128, 8192)))

    def slp(inputs):
        return np.concatenate([
            np.asarray(inputs[prefix + "_decay_slopes"], np.float32).reshape(-1),
            np.asarray(inputs[prefix + "_anchor_slopes"], np.float32).reshape(-1),
        ]).reshape(1, 32)

    def diag_row(a):
        return np.ascontiguousarray(np.diagonal(
            np.asarray(a, np.float32)).reshape(1, 32))

    names = {"win": prefix + "_W_in", "wout": prefix + "_W_out",
             "convk": (prefix + "_conv_k", cvk),
             "thetaB": (prefix + "_theta", th),
             "slopes": ("*", slp),
             "ssc": (prefix + "_score_scale",
                     lambda a: np.asarray(a, np.float32).reshape(32, 1).copy()),
             "fre": (prefix + "_W_re", diag_row),
             "fim": (prefix + "_W_im", diag_row),
             "nsc": (prefix + "_norm_scale",
                     lambda a: np.asarray(a, np.float32).reshape(1, 64).copy())}
    return nc, names



def emit_bcast_col(P, sc, src11, tok, name):
    scr = P.nc.dram_tensor("bc_" + name, [1, 1], F32, kind="Internal").ap()
    col = P.sb("col_" + name, [128, 1], F32)
    t = []

    def _(g):
        t.append(P.dma(g, "gpsimd", scr, src11, waits=(tok,)))
        t.append(P.dma(g, "gpsimd", col[:], scr.partition_broadcast(128).squeeze(1),
                       waits=(t[-1],)))
    P.sec("gpsimd", _)
    return col, t[-1]


def emit_allreduce_max(P, sc, vals, name):
    k = len(vals)
    cin = P.nc.dram_tensor("ccin_" + name, [1, k], F32, kind="Internal").ap()
    cout = P.nc.dram_tensor("ccout_" + name, [1, k], F32, kind="Internal").ap()
    row = P.sb("ccrow_" + name, [1, k], F32)
    toks = []

    def _(v):
        for i, (ap11, tok) in enumerate(vals):
            P.wait(v, "vector", tok)
            ins = v.tensor_scalar(out=row[0:1, i:i + 1], in0=ap11, scalar1=0.0,
                                  scalar2=None, op0=ALU.add, op1=ALU.bypass)
            if i == k - 1:
                toks.append(P.tick(ins, "vector"))
    P.sec("vector", _)

    def _(g):
        t1 = P.dma(g, "gpsimd", cin, row[:], waits=(toks[-1],))
        P.wait(g, "gpsimd", t1)
        ins = g.collective_compute(
            kind="AllReduce", op=ALU.max,
            replica_groups=[[0, 1, 2, 3, 4, 5, 6, 7]],
            ins=[cin], outs=[cout])
        tcc = P.tick(ins, "cc", 1)
        P.wait(g, "gpsimd", tcc)
        toks.append(P.dma(g, "gpsimd", row[:], cout))
    P.sec("gpsimd", _)
    return row, toks[-1]


def emit_colmax_fold(P, cs, sc, macc, tok, name):
    out11 = P.sb("m11_" + name, [1, 2], F32)
    PB = sc["PB"]
    t = []

    def _(tn):
        P.wait(tn, "tensor", tok, sc.get("psB_guard"))
        ins = tn.transpose(PB[0:1, 0:128], in_=macc, identity=cs["ident"][:])
        t.append(P.tick(ins, "tensor"))
    P.sec("tensor", _)

    def _(v):
        P.wait(v, "vector", t[-1])
        ins = v.tensor_reduce(out=out11[0:1, 0:1], in_=PB[0:1, 0:128], axis=AX.X,
                              op=ALU.max)
        tk = P.tick(ins, "vector")
        t.append(tk)
        sc["psB_guard"] = tk
    P.sec("vector", _)
    return out11[0:1, 0:1], t[-1]


def emit_dequant(P, cs, sc, wsrc, wdst, rows, cols, tag):
    stage = sc["stage"]
    wtmp = sc["wtmp"]
    wi32 = sc["wi32"]
    wbf = sc["wbf"]
    mx = sc["mx32"]
    sb_ = sc["sb32"]
    toks = []
    last_v = [sc.get("g_deq")]
    last_ld = [None]
    for r in range(rows // 128):
        for f0 in range(0, cols, 1024):
            ld = []

            def _(sy, r=r, f0=f0):
                ld.append(P.dma(sy, "sync", stage[:, 0:1024],
                                wsrc[r * 128:(r + 1) * 128, f0:f0 + 1024],
                                waits=(last_v[0],)))
                last_ld[0] = ld[-1]
            P.sec("sync", _)

            def _(v, r=r, f0=f0):
                P.wait(v, "vector", ld[-1])
                ins = v.tensor_reduce(out=mx[:], in_=stage[:, 0:1024].rearrange(
                    "p (b e) -> p b e", e=32), axis=AX.X, op=ALU.max,
                    apply_absolute_value=True)
                P.chain(v, "vector", ins)
                ins = v.tensor_scalar(out=sb_[:], in0=mx[:], scalar1=1.0 / 7.0,
                                      scalar2=None, op0=ALU.mult, op1=ALU.bypass)
                P.chain(v, "vector", ins)
                ins = v.reciprocal(out=mx[:], in_=mx[:])
                P.chain(v, "vector", ins)
                ins = v.tensor_tensor(
                    out=wtmp[:, 0:1024].rearrange("p (b e) -> p b e", e=32),
                    in0=stage[:, 0:1024].rearrange("p (b e) -> p b e", e=32),
                    in1=mx[:].unsqueeze(2).broadcast_to([128, 32, 32]),
                    op=ALU.mult)
                tk = P.tick(ins, "vector")
                last_v[0] = tk
            P.sec("vector", _)

            def _(s, r=r, f0=f0):
                P.wait(s, "scalar", last_v[0])
                ins = s.activation(out=wi32[:, 0:1024], in_=wtmp[:, 0:1024],
                                   func=AF.Copy, scale=7.0)
                toks.append(P.tick(ins, "scalar"))
            P.sec("scalar", _)

            def _(v, r=r, f0=f0):
                P.wait(v, "vector", toks[-1])
                ins = v.tensor_tensor(
                    out=wbf[:, 0:1024].rearrange("p (b e) -> p b e", e=32),
                    in0=wi32[:, 0:1024].rearrange("p (b e) -> p b e", e=32),
                    in1=sb_[:].unsqueeze(2).broadcast_to([128, 32, 32]),
                    op=ALU.mult)
                tk = P.tick(ins, "vector")
                last_v[0] = tk
            P.sec("vector", _)

            def _(g, r=r, f0=f0):
                toks.append(P.dma(g, "gpsimd",
                                  wdst[r * 128:(r + 1) * 128, f0:f0 + 1024],
                                  wbf[:, 0:1024], waits=(last_v[0],)))
            P.sec("gpsimd", _)
    sc["g_deq"] = last_v[0]
    return toks[-1]


def emit_ffn(P, cs, sc, W, Wdeq, tag):
    PA = sc["PA"]
    xn = sc["xn"]
    stage = sc["stage"]
    I = 3072
    NI = 24

    tq = [emit_dequant(P, cs, sc, W["wg"], Wdeq["wg_d"], I, D, tag + "g"),
          emit_dequant(P, cs, sc, W["wu"], Wdeq["wu_d"], I, D, tag + "u"),
          emit_dequant(P, cs, sc, W["wd"], Wdeq["wd_d"], D, I, tag + "d")]

    xn_toks = emit_ln(P, cs, sc, tag + "ln")
    macc = P.sb(tag + "_macc", [128, 1], F32)
    mt = []

    def _(v):
        P.wait(v, "vector", xn_toks)
        for c in range(NCH):
            if c == 0:
                ins = v.tensor_reduce(out=macc[:], in_=xn[:, 0:T], axis=AX.X,
                                      op=ALU.max, apply_absolute_value=True)
            else:
                ins = v.tensor_reduce(out=sc["mcol2"][:], in_=xn[:, c * T:(c + 1) * T],
                                      axis=AX.X, op=ALU.max, apply_absolute_value=True)
                P.chain(v, "vector", ins)
                ins = v.tensor_tensor(out=macc[:], in0=macc[:], in1=sc["mcol2"][:],
                                      op=ALU.max)
            tk = P.chain(v, "vector", ins)
            mt.append(tk)
    P.sec("vector", _)
    mx11, mtok = emit_colmax_fold(P, cs, sc, macc[:], mt[-1], tag + "x")
    gmax_row, gtok = emit_allreduce_max(P, sc, [(mx11, mtok)], tag + "c1")

    sxr = P.sb(tag + "_sxr", [1, 2], F32)
    st_ = []

    def _(v):
        P.wait(v, "vector", gtok)
        ins = v.tensor_scalar(out=sxr[0:1, 0:1], in0=gmax_row[0:1, 0:1],
                              scalar1=1.0 / 32767.0, scalar2=None,
                              op0=ALU.mult, op1=ALU.bypass)
        P.chain(v, "vector", ins)
        ins = v.reciprocal(out=sxr[0:1, 1:2], in_=sxr[0:1, 0:1])
        st_.append(P.tick(ins, "vector"))
    P.sec("vector", _)
    invx_col, invx_tok = emit_bcast_col(P, sc, sxr[0:1, 1:2], st_[-1], tag + "ix")

    qi32 = sc["qi32"]
    qx_toks = []
    qg = [None]
    for c in range(NCH):
        tq1 = []

        def _(s, c=c):
            P.wait(s, "scalar", invx_tok, xn_toks, qg[0])
            ins = s.activation(out=qi32[:, 0:T], in_=xn[:, c * T:(c + 1) * T],
                               func=AF.Copy, scale=invx_col[:, 0:1])
            tq1.append(P.tick(ins, "scalar"))
        P.sec("scalar", _)

        def _(v, c=c):
            P.wait(v, "vector", tq1[-1])
            ins = v.tensor_scalar(out=xn[:, c * T:(c + 1) * T], in0=qi32[:, 0:T],
                                  scalar1=0.0, scalar2=None, op0=ALU.add,
                                  op1=ALU.bypass)
            tk = P.tick(ins, "vector")
            qx_toks.append(tk)
            qg[0] = tk
        P.sec("vector", _)

    big = sc["big"]
    wlhsT = big[:, 0:NCH * I]
    obuf = sc["obuf"]
    evg = {}
    stog = {}
    mup11 = {}
    for wi, (wd_ap, store_ap) in enumerate(((Wdeq["wu_d"], Wdeq["up_s"]),
                                            (Wdeq["wg_d"], Wdeq["g2_s"]))):
        wt = []

        def _(sy, wi=wi, wd_ap=wd_ap):
            for c in range(NCH):
                wt.append(P.dma(sy, "sync", wlhsT[:, c * I:(c + 1) * I],
                                wd_ap[0:I, c * 128:(c + 1) * 128],
                                waits=(tq[wi], qx_toks, sc.get("g_big")),
                                transpose=True))
        P.sec("sync", _)
        wtok = wt[-1]
        macc2 = P.sb(tag + "_mac%d" % wi, [128, 1], F32)
        m2t = [None]
        for m in range(NI):
            for j in range(NT512):
                bidx = (m * NT512 + j) % 4
                bank = PA[:, bidx * 512:(bidx + 1) * 512]
                mm = []

                def _(t, m=m, j=j, bank=bank, bidx=bidx, wtok=wtok, wi=wi):
                    P.wait(t, "tensor", wtok, qx_toks, evg.get(bidx),
                           sc.get("psA_free"))
                    for c in range(NCH):
                        ins = t.matmul(bank,
                                       lhsT=wlhsT[:, c * I + m * 128: c * I + (m + 1) * 128],
                                       rhs=xn[:, c * T + j * 512: c * T + (j + 1) * 512],
                                       start=(c == 0), stop=(c == NCH - 1))
                    mm.append(P.tick(ins, "tensor"))
                P.sec("tensor", _)

                def _(s, m=m, j=j, bank=bank, bidx=bidx, mm=mm, wi=wi):
                    P.wait(s, "scalar", mm[-1], stog.get(bidx))
                    ins = s.activation(out=obuf[:, bidx * 512:(bidx + 1) * 512],
                                       in_=bank, func=AF.Copy)
                    evg[bidx] = P.tick(ins, "scalar")
                P.sec("scalar", _)

                def _(v, m=m, j=j, bidx=bidx, wi=wi):
                    P.wait(v, "vector", evg[bidx])
                    if m2t[0] is None:
                        ins = v.tensor_reduce(out=macc2[:],
                                              in_=obuf[:, bidx * 512:(bidx + 1) * 512],
                                              axis=AX.X, op=ALU.max,
                                              apply_absolute_value=True)
                    else:
                        ins = v.tensor_reduce(out=sc["mcol2"][:],
                                              in_=obuf[:, bidx * 512:(bidx + 1) * 512],
                                              axis=AX.X, op=ALU.max,
                                              apply_absolute_value=True)
                        P.chain(v, "vector", ins)
                        ins = v.tensor_tensor(out=macc2[:], in0=macc2[:],
                                              in1=sc["mcol2"][:], op=ALU.max)
                    m2t[0] = P.chain(v, "vector", ins)
                P.sec("vector", _)

                def _(s2, m=m, j=j, bidx=bidx, store_ap=store_ap):
                    tk = P.dma(s2, "scalar", store_ap[m * 128:(m + 1) * 128,
                                                      j * 512:(j + 1) * 512],
                               obuf[:, bidx * 512:(bidx + 1) * 512],
                               waits=(evg[bidx], m2t[0]))
                    stog[bidx] = tk
                P.sec("scalar", _)
        mup11[wi] = emit_colmax_fold(P, cs, sc, macc2[:], m2t[0], tag + "m%d" % wi)
    sc["g_big"] = P.tok("tensor")
    sc["g_obuf"] = list(stog.values())

    gmax2, g2tok = emit_allreduce_max(P, sc, [mup11[0], mup11[1]], tag + "c2")
    sr = P.sb(tag + "_sr", [1, 8], F32)
    st2 = []

    def _(v):
        P.wait(v, "vector", g2tok, st_[-1])
        ins = v.reciprocal(out=sr[0:1, 0:1], in_=gmax2[0:1, 0:1])
        P.chain(v, "vector", ins)
        ins = v.tensor_scalar(out=sr[0:1, 0:1], in0=sr[0:1, 0:1], scalar1=32767.0,
                              scalar2=None, op0=ALU.mult, op1=ALU.bypass)
        P.chain(v, "vector", ins)
        ins = v.reciprocal(out=sr[0:1, 1:2], in_=gmax2[0:1, 1:2])
        P.chain(v, "vector", ins)
        ins = v.tensor_scalar(out=sr[0:1, 1:2], in0=sr[0:1, 1:2], scalar1=32767.0,
                              scalar2=None, op0=ALU.mult, op1=ALU.bypass)
        P.chain(v, "vector", ins)
        ins = v.tensor_tensor(out=sr[0:1, 2:3], in0=gmax2[0:1, 1:2],
                              in1=sxr[0:1, 0:1], op=ALU.mult)
        P.chain(v, "vector", ins)
        ins = v.tensor_scalar(out=sr[0:1, 2:3], in0=sr[0:1, 2:3],
                              scalar1=1.0 / 32767.0, scalar2=None,
                              op0=ALU.mult, op1=ALU.bypass)
        st2.append(P.tick(ins, "vector"))
    P.sec("vector", _)
    invu_col, t_iu = emit_bcast_col(P, sc, sr[0:1, 0:1], st2[-1], tag + "iu")
    invg_col, t_ig = emit_bcast_col(P, sc, sr[0:1, 1:2], st2[-1], tag + "ig")
    sg_col, t_sg = emit_bcast_col(P, sc, sr[0:1, 2:3], st2[-1], tag + "sg")

    gbuf = sc["gbuf"]
    g2buf = sc["g2buf"]
    qs32 = sc["qs32"]
    qsf = sc["qsf"]
    macc3 = P.sb(tag + "_mac3", [128, 1], F32)
    m3t = [None]
    ldg = [None]
    for m in range(NI):
        ld = []

        def _(sy, m=m):
            ld.append(P.dma(sy, "sync", gbuf[:], Wdeq["g2_s"][m * 128:(m + 1) * 128, :],
                            waits=(ldg[0],)))
        P.sec("sync", _)
        tphase = []

        def _(s, m=m):
            P.wait(s, "scalar", ld[-1], t_ig, t_sg, m3t[0])
            ins = s.activation(out=qi32[:, 0:T], in_=gbuf[:], func=AF.Copy,
                               scale=invg_col[:, 0:1])
            P.chain(s, "scalar", ins)
            ins = s.activation(out=qsf[:, 0:T], in_=qi32[:, 0:T], func=AF.Sigmoid,
                               scale=sg_col[:, 0:1])
            P.chain(s, "scalar", ins)
            ins = s.activation(out=qs32[:, 0:T], in_=qsf[:, 0:T], func=AF.Copy,
                               scale=65536.0)
            tphase.append(P.tick(ins, "scalar"))
        P.sec("scalar", _)

        def _(v, m=m):
            P.wait(v, "vector", tphase[-1])
            ins = v.tensor_scalar(out=qsf[:, 0:T], in0=qs32[:, 0:T], scalar1=65535.0,
                                  scalar2=None, op0=ALU.min, op1=ALU.bypass)
            P.chain(v, "vector", ins)
            ins = v.tensor_tensor(out=g2buf[:], in0=qi32[:, 0:T], in1=qsf[:, 0:T],
                                  op=ALU.mult)
            P.chain(v, "vector", ins)
            ins = v.tensor_reduce(out=sc["mcol2"][:], in_=g2buf[:], axis=AX.X,
                                  op=ALU.max, apply_absolute_value=True)
            P.chain(v, "vector", ins)
            if m == 0:
                ins = v.tensor_scalar(out=macc3[:], in0=sc["mcol2"][:], scalar1=0.0,
                                      scalar2=None, op0=ALU.add, op1=ALU.bypass)
            else:
                ins = v.tensor_tensor(out=macc3[:], in0=macc3[:], in1=sc["mcol2"][:],
                                      op=ALU.max)
            m3t[0] = P.chain(v, "vector", ins)
        P.sec("vector", _)

        def _(g, m=m):
            tk = P.dma(g, "gpsimd", Wdeq["g2_s"][m * 128:(m + 1) * 128, :], g2buf[:],
                       waits=(m3t[0],))
            ldg[0] = tk
        P.sec("gpsimd", _)
    m3_11 = emit_colmax_fold(P, cs, sc, macc3[:], m3t[0], tag + "m3")
    gmax3, g3tok = emit_allreduce_max(P, sc, [m3_11], tag + "c3")

    st3 = []

    def _(v):
        P.wait(v, "vector", g3tok)
        ins = v.reciprocal(out=sr[0:1, 3:4], in_=gmax3[0:1, 0:1])
        P.chain(v, "vector", ins)
        ins = v.tensor_scalar(out=sr[0:1, 3:4], in0=sr[0:1, 3:4], scalar1=32767.0,
                              scalar2=None, op0=ALU.mult, op1=ALU.bypass)
        st3.append(P.tick(ins, "vector"))
    P.sec("vector", _)
    invg2_col, t_ig2 = emit_bcast_col(P, sc, sr[0:1, 3:4], st3[-1], tag + "ig2")

    oi = big[:, 0:NI * T]
    macc4 = P.sb(tag + "_mac4", [128, 1], F32)
    m4t = [None]
    ld4g = [None, None]
    for m in range(NI):
        ld = []

        def _(sy, m=m):
            ld.append(P.dma(sy, "sync", gbuf[:], Wdeq["g2_s"][m * 128:(m + 1) * 128, :],
                            waits=(ld4g[0], ldg[0])))
            ld.append(P.dma(sy, "sync", sc["ubuf"][:],
                            Wdeq["up_s"][m * 128:(m + 1) * 128, :],
                            waits=(ld4g[1], sc.get("g_obuf"))))
        P.sec("sync", _)
        t3 = []

        def _(s, m=m):
            P.wait(s, "scalar", ld[0], ld[1], t_ig2, t_iu, m4t[0])
            ins = s.activation(out=qi32[:, 0:T], in_=gbuf[:], func=AF.Copy,
                               scale=invg2_col[:, 0:1])
            P.chain(s, "scalar", ins)
            ins = s.activation(out=qs32[:, 0:T], in_=sc["ubuf"][:], func=AF.Copy,
                               scale=invu_col[:, 0:1])
            t3.append(P.tick(ins, "scalar"))
        P.sec("scalar", _)

        def _(v, m=m):
            P.wait(v, "vector", t3[-1], sc.get("g_big2"))
            ins = v.tensor_tensor(out=oi[:, m * T:(m + 1) * T], in0=qi32[:, 0:T],
                                  in1=qs32[:, 0:T], op=ALU.mult)
            P.chain(v, "vector", ins)
            ld4g[0] = P.tok("vector")
            ins = v.tensor_reduce(out=sc["mcol2"][:], in_=oi[:, m * T:(m + 1) * T],
                                  axis=AX.X, op=ALU.max, apply_absolute_value=True)
            P.chain(v, "vector", ins)
            if m == 0:
                ins = v.tensor_scalar(out=macc4[:], in0=sc["mcol2"][:], scalar1=0.0,
                                      scalar2=None, op0=ALU.add, op1=ALU.bypass)
            else:
                ins = v.tensor_tensor(out=macc4[:], in0=macc4[:], in1=sc["mcol2"][:],
                                      op=ALU.max)
            tk = P.chain(v, "vector", ins)
            m4t[0] = tk
            ld4g[1] = tk
        P.sec("vector", _)
    m4_11 = emit_colmax_fold(P, cs, sc, macc4[:], m4t[0], tag + "m4")
    gmax4, g4tok = emit_allreduce_max(P, sc, [m4_11], tag + "c4")

    st4 = []

    def _(v):
        P.wait(v, "vector", g4tok, st3[-1], st2[-1])
        ins = v.reciprocal(out=sr[0:1, 4:5], in_=gmax4[0:1, 0:1])
        P.chain(v, "vector", ins)
        ins = v.tensor_scalar(out=sr[0:1, 4:5], in0=sr[0:1, 4:5], scalar1=32767.0,
                              scalar2=None, op0=ALU.mult, op1=ALU.bypass)
        P.chain(v, "vector", ins)
        ins = v.tensor_tensor(out=sr[0:1, 5:6], in0=sxr[0:1, 0:1],
                              in1=gmax2[0:1, 0:1], op=ALU.mult)
        P.chain(v, "vector", ins)
        ins = v.tensor_scalar(out=sr[0:1, 5:6], in0=sr[0:1, 5:6],
                              scalar1=1.0 / 32767.0, scalar2=None,
                              op0=ALU.mult, op1=ALU.bypass)
        P.chain(v, "vector", ins)
        ins = v.tensor_tensor(out=sr[0:1, 6:7], in0=sr[0:1, 2:3],
                              in1=gmax3[0:1, 0:1], op=ALU.mult)
        P.chain(v, "vector", ins)
        ins = v.tensor_scalar(out=sr[0:1, 6:7], in0=sr[0:1, 6:7],
                              scalar1=1.0 / (65536.0 * 32767.0), scalar2=None,
                              op0=ALU.mult, op1=ALU.bypass)
        P.chain(v, "vector", ins)
        ins = v.tensor_tensor(out=sr[0:1, 7:8], in0=sr[0:1, 5:6],
                              in1=sr[0:1, 6:7], op=ALU.mult)
        P.chain(v, "vector", ins)
        ins = v.tensor_tensor(out=sr[0:1, 7:8], in0=sr[0:1, 7:8],
                              in1=gmax4[0:1, 0:1], op=ALU.mult)
        P.chain(v, "vector", ins)
        ins = v.tensor_scalar(out=sr[0:1, 7:8], in0=sr[0:1, 7:8],
                              scalar1=1.0 / 32767.0, scalar2=None,
                              op0=ALU.mult, op1=ALU.bypass)
        st4.append(P.tick(ins, "vector"))
    P.sec("vector", _)
    invoi_col, t_io = emit_bcast_col(P, sc, sr[0:1, 4:5], st4[-1], tag + "io")
    so_col, t_so = emit_bcast_col(P, sc, sr[0:1, 7:8], st4[-1], tag + "so")

    qo_toks = []
    qog = [None]
    for m in range(NI):
        tq2 = []

        def _(s, m=m):
            P.wait(s, "scalar", t_io, m4t[0], qog[0])
            ins = s.activation(out=qi32[:, 0:T], in_=oi[:, m * T:(m + 1) * T],
                               func=AF.Copy, scale=invoi_col[:, 0:1])
            tq2.append(P.tick(ins, "scalar"))
        P.sec("scalar", _)

        def _(v, m=m):
            P.wait(v, "vector", tq2[-1])
            ins = v.tensor_scalar(out=oi[:, m * T:(m + 1) * T], in0=qi32[:, 0:T],
                                  scalar1=0.0, scalar2=None, op0=ALU.add,
                                  op1=ALU.bypass)
            tk = P.tick(ins, "vector")
            qo_toks.append(tk)
            qog[0] = tk
        P.sec("vector", _)

    xsrc, xdst = sc["xsrc"], sc["xdst"]
    wdT = sc["wdT"]
    res_toks = []
    wdg = {}
    evg2 = {}
    stg2 = [None]
    for half in range(2):
        for n in range(NT512):
            mm_last = {}
            for k in range(NI):
                kb = k % 2
                wtl = []

                def _(sy, half=half, n=n, k=k, kb=kb):
                    if n == 0 or True:
                        wtl.append(P.dma(sy, "sync", wdT[:, kb * D:(kb + 1) * D],
                                         Wdeq["wd_d"][0:D, k * 128:(k + 1) * 128],
                                         waits=(tq[2], wdg.get(kb), qo_toks, ldg[0]),
                                         transpose=True))
                P.sec("sync", _)

                def _(t, half=half, n=n, k=k, kb=kb, wtl=wtl):
                    P.wait(t, "tensor", wtl[-1] if wtl else None, qo_toks,
                           evg2.get("all") if k == 0 else None,
                           sc.get("psA_free"))
                    for m in range(4):
                        mi = half * 4 + m
                        ins = t.matmul(PA[:, m * 512:(m + 1) * 512],
                                       lhsT=wdT[:, kb * D + mi * 128: kb * D + (mi + 1) * 128],
                                       rhs=oi[:, k * T + n * 512: k * T + (n + 1) * 512],
                                       start=(k == 0), stop=(k == NI - 1))
                        if k == NI - 1:
                            mm_last[m] = P.tick(ins, "tensor")
                    wdg[kb] = P.tok("tensor")
                P.sec("tensor", _)

            ld2 = []

            def _(sy, half=half, n=n):
                for m in range(4):
                    mi = half * 4 + m
                    ld2.append(P.dma(sy, "sync",
                                     stage[:, m * 512:(m + 1) * 512],
                                     xsrc[:, mi * T + n * 512: mi * T + (n + 1) * 512],
                                     waits=(stg2[0], sc.get("g_stage"))))
            P.sec("sync", _)
            vt2 = []

            def _(v, half=half, n=n, ld2=ld2, mm_last=mm_last):
                P.wait(v, "vector", ld2, t_so, *mm_last.values())
                for m in range(4):
                    ins = v.scalar_tensor_tensor(
                        out=stage[:, m * 512:(m + 1) * 512],
                        in0=PA[:, m * 512:(m + 1) * 512], scalar=so_col[:, 0:1],
                        in1=stage[:, m * 512:(m + 1) * 512],
                        op0=ALU.mult, op1=ALU.add)
                    vt2.append(P.tick(ins, "vector"))
                evg2["all"] = vt2[-4:]
            P.sec("vector", _)

            def _(g, half=half, n=n, vt2=vt2):
                for m in range(4):
                    mi = half * 4 + m
                    tk = P.dma(g, "gpsimd",
                               xdst[:, mi * T + n * 512: mi * T + (n + 1) * 512],
                               stage[:, m * 512:(m + 1) * 512], waits=(vt2[m - 4],))
                    res_toks.append(tk)
                stg2[0] = res_toks[-1]
            P.sec("gpsimd", _)
    sc["g_stage"] = None
    sc["g_big2"] = None
    return res_toks



def _decl_consts(nc):
    cin = {}
    for nm, arr in CONSTS.items():
        cin[nm] = nc.dram_tensor(nm, list(arr.shape), F32, kind="ExternalInput").ap()
    return cin


def _base_prog(weight_names):
    nc = bass.Bass("TRN2", target_bir_lowering=False)
    cin = _decl_consts(nc)
    xin = nc.dram_tensor("xin", [128, NCH * T], F32, kind="ExternalInput").ap()
    xout = nc.dram_tensor("xout", [128, NCH * T], F32, kind="ExternalOutput").ap()
    consts_needed = weight_names.pop("_consts")
    W = {}
    for nm, shape in weight_names.items():
        W[nm] = nc.dram_tensor(nm, list(shape), F32, kind="ExternalInput").ap()
    P = Prog(nc)
    sc = {}
    sc["xsrc"] = xin
    sc["xdst"] = xout
    sc["xn"] = P.sb("xn", [128, NCH * T], BF16)
    sc["stage"] = P.sb("stage", [128, T], F32)
    sc["t_bf"] = P.sb("t_bf", [128, T], BF16)
    sc["t_f32"] = P.sb("t_f32", [128, 1024], F32)
    sc["abc_s"] = P.sb("abc_s", [128, T], BF16)
    sc["bbc_s"] = P.sb("bbc_s", [128, T], BF16)
    sc["R0"] = P.sb("R0", [1, T], F32)
    sc["R1"] = P.sb("R1", [1, T], F32)
    sc["Rb0"] = P.sb("Rb0", [1, T], BF16)
    sc["Rb1"] = P.sb("Rb1", [1, T], BF16)
    sc["PA"] = P.ps("PA", [128, 2048], F32)
    sc["PB"] = P.ps("PB", [128, 1536], F32)
    sc["PTb"] = P.ps("PTb", [128, 1024], BF16)
    P.begin()
    cs = emit_consts(P, cin, consts_needed)
    return nc, P, cs, sc, W


def _finish_prog(P, res_toks, sc=None, dbg=None):
    def _(g):
        P.wait(g, "gpsimd", res_toks, P.tok("gpsimd"),
               P.tok("vector"), P.tok("scalar"), P.tok("tensor"))
        if dbg:
            for nm, ap in dbg.items():
                P.dma(g, "gpsimd", dbg_outs[nm], ap)
        P.wait(g, "gpsimd", P.tok("gpsimd"), getattr(P, "dbg_toks", None))
    dbg_outs = {}
    if dbg:
        for nm, ap in dbg.items():
            shp = list(ap.shape)
            dt = ap.dtype
            dbg_outs[nm] = P.nc.dram_tensor("dbg_" + nm, shp, dt, kind="ExternalOutput").ap()
    P.sec("gpsimd", _)
    P.stack.close()


def build_attn_b(prefix):
    wn = {"wq": (D, D), "wk": (D, D), "wv": (D, D), "wo": (D, D),
          "_consts": LN_CONSTS + ["upper01"]}
    nc, P, cs, sc, W = _base_prog(wn)
    res = emit_attn_b(P, cs, sc, W)
    _finish_prog(P, res, sc, sc.get("dbg") if DEBUG else None)
    names = {"wq": prefix + "_Wq", "wk": prefix + "_Wk", "wv": prefix + "_Wv",
             "wo": prefix + "_Wo"}
    return nc, names


def build_ffn(prefix):
    I = 3072
    wn = {"wg": (I, D), "wu": (I, D), "wd": (D, I),
          "_consts": LN_CONSTS + ["ident"]}
    nc, P, cs, sc, W = _base_prog(wn)
    sc["big"] = P.sb("big", [128, 24 * T], BF16)
    sc["obuf"] = P.sb("obuf", [128, 4 * 512], BF16)
    sc["gbuf"] = P.sb("gbuf", [128, T], BF16)
    sc["g2buf"] = P.sb("g2buf", [128, T], BF16)
    sc["qi32"] = P.sb("qi32", [128, T], I32)
    sc["qs32"] = P.sb("qs32", [128, T], I32)
    sc["qsf"] = sc["stage"]
    sc["mx32"] = P.sb("mx32", [128, 32], F32)
    sc["sb32"] = P.sb("sb32", [128, 32], F32)
    sc["mcol2"] = P.sb("mcol2", [128, 1], F32)
    sc["wdT"] = sc["g2buf"]
    sc["ubuf"] = sc["obuf"][:, 0:T]
    sc["wtmp"] = sc["t_f32"]
    sc["wi32"] = sc["qi32"]
    sc["wbf"] = sc["g2buf"]
    Wdeq = {
        "wg_d": nc.dram_tensor("wg_d", [I, D], BF16, kind="Internal").ap(),
        "wu_d": nc.dram_tensor("wu_d", [I, D], BF16, kind="Internal").ap(),
        "wd_d": nc.dram_tensor("wd_d", [D, I], BF16, kind="Internal").ap(),
        "up_s": nc.dram_tensor("up_s", [I, T], BF16, kind="Internal").ap(),
        "g2_s": nc.dram_tensor("g2_s", [I, T], BF16, kind="Internal").ap(),
    }
    res = emit_ffn(P, cs, sc, W, Wdeq, "f")
    _finish_prog(P, res, sc, sc.get("dbg") if DEBUG else None)
    names = {"wg": prefix + "_w_gate", "wu": prefix + "_w_up",
             "wd": prefix + "_w_down"}
    return nc, names



def _state_from_x(xb):
    return np.ascontiguousarray(
        xb.T.reshape(NCH, 128, T).transpose(1, 0, 2).reshape(128, NCH * T))


def _x_from_state(st):
    return np.ascontiguousarray(
        st.reshape(128, NCH, T).transpose(1, 0, 2).reshape(D, T).T)


_CACHE = {}
LAST_RESULTS = None


def _run(nc, names, inputs, states, extra=None):
    in_maps = []
    for b in range(8):
        m = dict(CONSTS)
        m["xin"] = states[b]
        for local, ref in names.items():
            if isinstance(ref, tuple):
                if ref[0] == "*":
                    m[local] = ref[1](inputs)
                else:
                    m[local] = ref[1](inputs[ref[0]])
            else:
                m[local] = np.ascontiguousarray(np.asarray(inputs[ref], np.float32))
        if extra:
            m.update(extra[b])
        in_maps.append(m)
    res = run_bass_kernel_spmd(nc, in_maps, core_ids=list(range(8)))
    global LAST_RESULTS
    LAST_RESULTS = res.results
    return [r["xout"] for r in res.results]


BUILDERS = {"attn_b": build_attn_b, "ffn": build_ffn, "attn_a": build_attn_a, "mamba": build_mamba, "l1b": build_l1b}

CHAIN = [("attn_a", "L0_attn_a"), ("ffn", "L0_ffn"),
         ("attn_b", "L0_attn_b"), ("ffn", "L0_ffn"),
         ("mamba", "L1_attn_a"), ("ffn", "L1_ffn"),
         ("l1b", "L1_attn_b"), ("ffn", "L1_ffn")]


def kernel(**inputs):
    x = np.asarray(inputs["L0_attn_a_x"], np.float32)
    states = [_state_from_x(x[b]) for b in range(8)]

    for kind, prefix in CHAIN:
        key = kind + ":" + prefix
        if key not in _CACHE:
            _CACHE[key] = BUILDERS[kind](prefix)
        nc, names = _CACHE[key]
        states = _run(nc, names, inputs, states)

    out = np.stack([_x_from_state(s) for s in states]).astype(np.float32)
    return out
